# version 14
# baseline (speedup 1.0000x reference)
import dataclasses
import numpy as np
import ml_dtypes

import concourse.bass as bass
import concourse.mybir as mybir
from concourse.bass_utils import run_bass_kernel_spmd

F32 = mybir.dt.float32
BF16 = mybir.dt.bfloat16
AF = mybir.ActivationFunctionType
ALU = mybir.AluOpType
AX = mybir.AxisListType

K, C, D = 64, 64, 1024

TEN, VEC, SCL, GPS, SYN = "tensor", "vector", "scalar", "gpsimd", "sync"


class Sem:
    def __init__(self, h):
        self.h = h
        self.n = 0


DMA_DEPTH = 4


def build_nc(T=8192, n_cores=8, phases=4, nblk=10**9, nstg=99):
    NT = T // 128
    NB = T // 512
    OWNT = T // 2
    NOB = OWNT // 512
    PAIR_GROUPS = [[i, i + 1] for i in range(0, n_cores, 2)]
    nc = bass.Bass("TRN2", target_bir_lowering=False, debug=False, num_devices=n_cores)

    xTM = nc.dram_tensor("xTM", [T, D], F32, kind="ExternalInput").ap()
    wq = nc.dram_tensor("wq", [D, 512], F32, kind="ExternalInput").ap()
    wk = nc.dram_tensor("wk", [D, 512], F32, kind="ExternalInput").ap()
    wv = nc.dram_tensor("wv", [D, 512], F32, kind="ExternalInput").ap()
    wg = nc.dram_tensor("wg", [D, 512], F32, kind="ExternalInput").ap()
    wsm = nc.dram_tensor("wsm", [D, 24], F32, kind="ExternalInput").ap()
    wo = nc.dram_tensor("wo", [D, D], F32, kind="ExternalInput").ap()
    weT = nc.dram_tensor("weT", [8, D, D], F32, kind="ExternalInput").ap()
    bexp = nc.dram_tensor("bexp", [8, D], F32, kind="ExternalInput").ap()
    dtb = nc.dram_tensor("dtb", [128, 8], F32, kind="ExternalInput").ap()
    alog = nc.dram_tensor("alog", [128, 8], F32, kind="ExternalInput").ap()
    brr = nc.dram_tensor("brr", [128, 8], F32, kind="ExternalInput").ap()
    nwr = nc.dram_tensor("nwr", [128, 64], F32, kind="ExternalInput").ap()
    c_triu = nc.dram_tensor("c_triu", [128, 64], BF16, kind="ExternalInput").ap()
    c_ustr = nc.dram_tensor("c_ustr", [128, 64], BF16, kind="ExternalInput").ap()
    c_strict = nc.dram_tensor("c_strict", [128, 64], BF16, kind="ExternalInput").ap()
    c_trilT = nc.dram_tensor("c_trilT", [128, 64], BF16, kind="ExternalInput").ap()
    csum0 = nc.dram_tensor("csum0", [128, 128], BF16, kind="ExternalInput").ap()
    csum1 = nc.dram_tensor("csum1", [128, 128], BF16, kind="ExternalInput").ap()
    cswp = nc.dram_tensor("cswp", [128, 128], BF16, kind="ExternalInput").ap()
    c_i128 = nc.dram_tensor("c_i128", [128, 128], BF16, kind="ExternalInput").ap()
    c_i64s = nc.dram_tensor("c_i64s", [128, 64], F32, kind="ExternalInput").ap()
    epsc = nc.dram_tensor("epsc", [128, 6], F32, kind="ExternalInput").ap()

    ogall = nc.dram_tensor("ogall", [512, T], BF16, kind="Internal").ap()
    xcs = nc.dram_tensor("xcs", [128, 8, T], BF16, kind="Internal").ap()
    rww_d = nc.dram_tensor("rww_d", [NT, 128, 8], F32, kind="Internal").ap()
    attn_d = nc.dram_tensor("attn_d", [OWNT // 128, 128, D], BF16, kind="Internal").ap()
    xOWN = nc.dram_tensor("xOWN", [OWNT, D], F32, kind="ExternalInput").ap()
    og_own = nc.dram_tensor("og_own", [512, OWNT], BF16, kind="Internal").ap()
    og_par = nc.dram_tensor("og_par", [512, OWNT], BF16, kind="Internal").ap()
    xcs_own = nc.dram_tensor("xcs_own", [128, 8, OWNT], BF16, kind="Internal").ap()
    rww_own = nc.dram_tensor("rww_own", [NT // 2, 128, 8], F32, kind="Internal").ap()
    cc_in = nc.dram_tensor("cc_in", [512 * OWNT], BF16, kind="Internal").ap()
    cc_out = nc.dram_tensor("cc_out", [2 * 512 * OWNT], BF16, kind="Internal").ap()
    outY = nc.dram_tensor("outY", [OWNT, D], F32, kind="ExternalOutput").ap()

    from contextlib import ExitStack
    es = ExitStack()
    SB = lambda name, shape, dt=F32: es.enter_context(nc.sbuf_tensor(name, shape, dt))
    PS = lambda name, shape, dt=F32: es.enter_context(nc.psum_tensor(name, shape, dt))
    SEM = lambda name: Sem(es.enter_context(nc.semaphore(name)))

    with es:
        k_triu = SB("k_triu", [128, 64], BF16)
        k_ustr = SB("k_ustr", [128, 64], BF16)
        k_strict = SB("k_strict", [128, 64], BF16)
        k_trilT = SB("k_trilT", [128, 64], BF16)
        k_sum0 = SB("k_sum0", [128, 128], BF16)
        k_sum1 = SB("k_sum1", [128, 128], BF16)
        k_swp = SB("k_swp", [128, 128], BF16)
        k_i128 = SB("k_i128", [128, 128], BF16)
        k_i64s = SB("k_i64s", [128, 64], F32)
        k_eps = SB("k_eps", [128, 6], F32)
        k_dtb = SB("k_dtb", [128, 8], F32)
        k_alog = SB("k_alog", [128, 8], F32)
        k_eA = SB("k_eA", [128, 8], F32)
        k_brr = SB("k_brr", [128, 8], F32)
        k_nw = SB("k_nw", [128, 64], F32)
        W_q = SB("W_q", [128, 8, 512], BF16)
        W_k = SB("W_k", [128, 8, 512], BF16)
        W_v = SB("W_v", [128, 8, 512], BF16)
        W_g = SB("W_g", [128, 8, 512], BF16)
        W_sm = SB("W_sm", [128, 8, 24], BF16)
        W_e = SB("W_e", [128, 2, 8, 1024], BF16)
        B_e = SB("B_e", [8, 1024], BF16)
        rstdc = SB("rstdc", [128, NT], F32)
        S_st = SB("S_st", [128, 512], F32)
        S_bf = SB("S_bf", [128, 512], BF16)
        Stmp = SB("Stmp", [128, 512], F32)
        eglr = SB("eglr", [128, 2, 2, 8], F32)
        xtm = SB("xtm", [128, 2, 1024], BF16)
        xsq = SB("xsq", [128, 1024], BF16)
        xsq2 = SB("xsq2", [128, 512], BF16)
        stat = SB("stat", [128, 16], F32)
        xcb = SB("xcb", [128, 8, 512], BF16)
        xctm = SB("xctm", [128, 1024], BF16)
        qn_b = SB("qn_b", [128, 2, 512], BF16)
        kn_b = SB("kn_b", [128, 2, 512], BF16)
        vb_b = SB("vb_b", [128, 2, 512], BF16)
        kbg_b = SB("kbg_b", [128, 2, 512], BF16)
        qe_b = SB("qe_b", [128, 2, 512], BF16)
        ke2 = SB("ke2", [128, 2, 8, 128], BF16)
        gt_b = SB("gt_b", [128, 2, 512], BF16)
        qnK = SB("qnK", [128, 4, 128], BF16)
        qeK = SB("qeK", [128, 4, 128], BF16)
        knK = SB("knK", [128, 4, 128], BF16)
        qnKd = SB("qnKd", [128, 4, 128], BF16)
        qeKd = SB("qeKd", [128, 4, 128], BF16)
        knKd = SB("knKd", [128, 4, 128], BF16)
        beta_b = SB("beta_b", [128, 2, 8], F32)
        dec_b = SB("dec_b", [128, 2, 8], F32)
        dec_bf = SB("dec_bf", [128, 2, 8], BF16)
        eg_b = SB("eg_b", [128, 8], F32)
        eglg_b = SB("eglg_b", [128, 8], F32)
        scl_b = SB("scl_b", [128, 16], F32)
        rhsU = SB("rhsU", [128, 512], BF16)
        rhsL = SB("rhsL", [128, 512], BF16)
        Lst = SB("Lst", [128, 512], BF16)
        LTm = SB("LTm", [128, 512], BF16)
        attT = SB("attT", [128, 512], BF16)
        Pk_s = SB("Pk_s", [128, 512], BF16)
        PkT_s = SB("PkT_s", [128, 512], BF16)
        Av_s = SB("Av_s", [128, 512], BF16)
        AvT_s = SB("AvT_s", [128, 512], BF16)
        u_s = SB("u_s", [128, 512], BF16)
        w_s = SB("w_s", [128, 512], BF16)
        vn0 = SB("vn0", [128, 512], BF16)
        vn1 = SB("vn1", [128, 512], BF16)
        o_b = SB("o_b", [128, 512], BF16)
        ms_b = SB("ms_b", [128, 8], F32)
        og_b = SB("og_b", [128, 512], BF16)
        atile = SB("atile", [128, 1024], BF16)
        ogf_b = SB("ogf_b", [128, 4, 512], BF16)
        xtf = SB("xtf", [128, 1024], F32)
        ffa = SB("ffa", [128, 4, 1024], F32)
        o_t1 = SB("o_t1", [128, 1024], F32)
        rww_t = SB("rww_t", [128, 8], F32)
        rwT_ob = SB("rwT_ob", [8, 128], BF16)
        rwwb_bf = SB("rwwb_bf", [128, 4, 8], BF16)
        sm_t = SB("sm_t", [128, 8], F32)
        rwwb = SB("rwwb", [128, 4, 8], F32)
        rt4 = SB("rt4", [128, 4, 8], F32)
        rt4b = SB("rt4b", [128, 4, 8], F32)
        sm_u = SB("sm_u", [128, 8], F32)
        sm_m = SB("sm_m", [128, 2], F32)

        p0 = PS("p0", [128, 512])
        p1 = PS("p1", [128, 512])
        p2 = PS("p2", [128, 512])
        p3 = PS("p3", [128, 512])
        p4 = PS("p4", [128, 512])
        p5 = PS("p5", [128, 512])
        p6 = PS("p6", [128, 512])
        p7 = PS("p7", [128, 512])
        banks = [p0, p1, p2, p3, p4, p5, p6, p7]

        sLD = SEM("sLD")
        sGD = SEM("sGD")
        sT = SEM("sT")
        sV = SEM("sV")
        sSc = SEM("sSc")
        sCC = SEM("sCC")
        sST = SEM("sST")
        sGP = SEM("sGP")

        block = es.enter_context(nc.Block())

        def program(eng):
            for _s in (sLD, sGD, sT, sV, sSc, sST, sCC, sGP):
                _s.n = 0
            E = {"sync": nc.sync, "gpsimd": nc.gpsimd, "vector": nc.vector,
                 "scalar": nc.scalar, "tensor": nc.tensor}[eng]
            me = eng
            v, sc, gp, sy = nc.vector, nc.scalar, nc.gpsimd, nc.sync

            class _T:
                @staticmethod
                def matmul(*a, **kw):
                    kw.setdefault("skip_group_check", True)
                    return nc.tensor.matmul(*a, **kw)

                @staticmethod
                def transpose(*a, **kw):
                    return nc.tensor.transpose(*a, **kw)
            t = _T()
            p6b = p6[:].bitcast(BF16)
            p4b = p4[:].bitcast(BF16)

            def dma(qeng, out, in_, sem=None, depth=None):
                depth = depth or DMA_DEPTH
                sem = sem or (sGD if qeng == GPS else sLD)
                if me == qeng:
                    E.wait_ge(sem.h, max(0, sem.n - 16 * (depth - 1)))
                    E.dma_start(out=out, in_=in_).then_inc(sem.h, 16)
                sem.n += 16

            def dmaop(qeng, fn, sem=None, depth=None):
                depth = depth or DMA_DEPTH
                sem = sem or (sGD if qeng == GPS else sLD)
                if me == qeng:
                    E.wait_ge(sem.h, max(0, sem.n - 16 * (depth - 1)))
                    fn().then_inc(sem.h, 16)
                sem.n += 16

            own_sem = {VEC: sV, SCL: sSc, GPS: sGP, TEN: sT}

            def op(oeng, fn, sem=None, k=1):
                if me == oeng:
                    if oeng in (VEC, SCL, GPS):
                        E.wait_ge(own_sem[oeng].h, own_sem[oeng].n)
                    ins = fn()
                    if sem is not None:
                        ins.then_inc(sem.h, k)
                if sem is not None:
                    sem.n += k

            def wait(weng, sem, val=None):
                if me == weng:
                    E.wait_ge(sem.h, sem.n if val is None else val)

            def bck(ap, nrep):
                return ap.unsqueeze(1).broadcast_to(
                    [ap.shape[0], nrep, ap.shape[1]])

            def bch(ap, rep):
                return ap.unsqueeze(2).broadcast_to(
                    [ap.shape[0], ap.shape[1], rep])

            for (dst, src) in [(k_triu, c_triu), (k_ustr, c_ustr),
                               (k_strict, c_strict), (k_trilT, c_trilT),
                               (k_sum0, csum0), (k_sum1, csum1),
                               (k_swp, cswp), (k_i128, c_i128),
                               (k_i64s, c_i64s), (k_eps, epsc), (k_dtb, dtb), (k_alog, alog),
                               (k_brr, brr), (k_nw, nwr)]:
                dma(SYN, dst[:], src)
            for (dst, src) in [(W_q, wq), (W_k, wk), (W_v, wv), (W_g, wg)]:
                for dc in range(8):
                    dma(GPS, dst[:, dc, :], src[dc * 128:(dc + 1) * 128, :])
            for dc in range(8):
                dma(GPS, W_sm[:, dc, :], wsm[dc * 128:(dc + 1) * 128, :])
            dma(GPS, B_e[:], bexp)
            wq_g = sGD.n
            wait(GPS, sGD, wq_g)
            kload = sLD.n
            wait(SCL, sLD, kload)
            op(SCL, lambda: sc.activation(k_eA[:], k_alog[:], AF.Exp), sSc)
            op(VEC, lambda: v.memset(S_st[:], 0.0), sV)
            op(VEC, lambda: v.memset(S_bf[:], 0.0), sV)
            op(VEC, lambda: v.memset(vn0[64:128, :], 0.0), sV)
            op(VEC, lambda: v.memset(vn1[0:64, :], 0.0), sV)

            fst = {}

            def front(ti):
                pb = ti % 2
                sl = ti % 2
                if ti + 1 < NTI:
                    wait(GPS, sV)
                    wait(GPS, sSc)
                    dma(GPS, xtm[:, (ti + 1) % 2, :],
                        xTM[(ti + 1) * 128: (ti + 2) * 128, :])
                xload = fst[ti]["xl"]
                wait(SCL, sV)
                wait(SCL, sGD, xload)
                op(SCL, lambda sl=sl: sc.activation(
                    xsq[:], xtm[:, sl, :], AF.Square,
                    accum_out=stat[:, 1:2]), sSc)
                wait(VEC, sGD, xload)
                op(VEC, lambda sl=sl: v.tensor_reduce(
                    stat[:, 0:1], xtm[:, sl, :], AX.X, ALU.add), sV)
                wait(VEC, sSc)
                op(VEC, lambda: v.tensor_scalar(stat[:, 2:3], stat[:, 0:1],
                                                1.0 / D, None, ALU.mult), sV)
                op(VEC, lambda: v.scalar_tensor_tensor(
                    stat[:, 3:4], stat[:, 2:3], 1.0, stat[:, 2:3],
                    ALU.mult, ALU.mult), sV)
                op(VEC, lambda: v.scalar_tensor_tensor(
                    stat[:, 5:6], stat[:, 1:2], 1.0 / D, stat[:, 3:4],
                    ALU.mult, ALU.subtract), sV)
                wait(SCL, sV)
                op(SCL, lambda: sc.activation(
                    stat[:, 6:7], stat[:, 5:6], AF.Sqrt,
                    bias=k_eps[:, 4:5]), sSc)
                wait(VEC, sSc)
                op(VEC, lambda ti=ti: v.reciprocal(rstdc[:, ti:ti + 1],
                                                   stat[:, 6:7]), sV)
                wait(VEC, sT)
                op(VEC, lambda sl=sl: v.tensor_scalar(
                    xctm[:], xtm[:, sl, :], stat[:, 2:3], None,
                    ALU.subtract), sV)
                xc_v = sV.n
                wait(TEN, sV, xc_v)
                i4 = ti % 4
                for half in range(2):
                    wait(TEN, sV)
                    wait(VEC, sLD)
                    for dq in range(4):
                        dc = half * 4 + dq
                        op(TEN, lambda dc=dc, dq=dq: t.transpose(
                            p4b[:, dq * 128:(dq + 1) * 128],
                            xctm[:, dc * 128:(dc + 1) * 128], k_i128[:]),
                           sT if dq == 3 else None)
                    tp = sT.n
                    wait(VEC, sT, tp)
                    op(VEC, lambda half=half, i4=i4: v.tensor_copy(
                        xcb[:, half * 4:(half + 1) * 4,
                            i4 * 128:(i4 + 1) * 128],
                        p4b[:, 0:512].rearrange("p (c n) -> p c n", n=128)),
                       sV)
                xcfm_v = sV.n
                wait(SYN, sV, xcfm_v)
                dma(SYN, xcs[:, :, ti * 128: (ti + 1) * 128],
                    xcb[:, :, i4 * 128:(i4 + 1) * 128])

                if nstg < 2:
                    return
                wait(TEN, sGD, wq_g)
                wait(TEN, sV, xcfm_v)
                for dc in range(8):
                    st_, sp_ = dc == 0, dc == 7
                    lhs = xcb[:, dc, i4 * 128:(i4 + 1) * 128]
                    for (pk, Wt) in ((p0, W_q), (p1, W_k), (p2, W_v),
                                     (p3, W_g)):
                        op(TEN, lambda pk=pk, Wt=Wt, lhs=lhs, dc=dc,
                           st_=st_, sp_=sp_: t.matmul(
                            pk[:], lhs, Wt[:, dc, :], start=st_, stop=sp_),
                           None)
                    op(TEN, lambda lhs=lhs, dc=dc, st_=st_, sp_=sp_:
                       t.matmul(p4[:, 0:24], lhs, W_sm[:, dc, :],
                                start=st_, stop=sp_),
                       sT if sp_ else None)
                proj_t = sT.n

                if nstg < 3:
                    return
                wait(SCL, sT, proj_t)
                wait(SCL, sV)
                op(SCL, lambda ti=ti, pb=pb: sc.activation(
                    beta_b[:, pb, :], p4[:, 0:8], AF.Sigmoid,
                    scale=rstdc[:, ti:ti + 1]), sSc)
                wait(VEC, sT, proj_t)
                op(VEC, lambda ti=ti: v.scalar_tensor_tensor(
                    sm_t[:], p4[:, 8:16], rstdc[:, ti:ti + 1], k_dtb[:],
                    ALU.mult, ALU.add), sV)
                wait(SCL, sV)
                wait(SCL, sGP)
                op(SCL, lambda: sc.activation(sm_u[:], sm_t[:], AF.Abs), sSc)
                op(SCL, lambda: sc.activation(sm_u[:], sm_u[:], AF.Exp,
                                              scale=-1.0), sSc)
                op(SCL, lambda: sc.activation(sm_u[:], sm_u[:], AF.Ln,
                                              bias=k_eps[:, 2:3]), sSc)
                wait(VEC, sSc)
                op(VEC, lambda: v.tensor_scalar_max(sm_t[:], sm_t[:], 0.0), sV)
                op(VEC, lambda: v.tensor_add(sm_t[:], sm_t[:], sm_u[:]), sV)
                op(VEC, lambda pb=pb: v.tensor_mul(dec_b[:, pb, :], sm_t[:],
                                                   k_eA[:]), sV)
                op(VEC, lambda pb=pb: v.tensor_copy(dec_bf[:, pb, :],
                                                    dec_b[:, pb, :]), sV)
                dec_v = sV.n
                if i4 == 0:
                    wait(VEC, sLD)
                op(VEC, lambda ti=ti, i4=i4: v.scalar_tensor_tensor(
                    rwwb[:, i4, :], p4[:, 16:24], rstdc[:, ti:ti + 1],
                    k_brr[:], ALU.mult, ALU.add), sV)
                if i4 == 3:
                    m4a = stat[:, 0:4].unsqueeze(2)
                    m4b = stat[:, 4:8].unsqueeze(2)
                    op(VEC, lambda: v.tensor_reduce(
                        m4a, rwwb[:], AX.X, ALU.max, negate=True), sV)
                    op(VEC, lambda: v.tensor_tensor(
                        rt4[:], rwwb[:], bch(stat[:, 0:4], 8), ALU.add), sV)
                    wait(SCL, sV)
                    op(SCL, lambda: sc.activation(
                        rt4[:].rearrange("p a b -> p (a b)"),
                        rt4[:].rearrange("p a b -> p (a b)"), AF.Exp), sSc)
                    wait(VEC, sSc)
                    op(VEC, lambda: v.tensor_reduce(
                        m4b, rt4[:], AX.X, ALU.add), sV)
                    op(VEC, lambda: v.reciprocal(stat[:, 4:8],
                                                 stat[:, 4:8]), sV)
                    op(VEC, lambda: v.tensor_tensor(
                        rt4[:], rt4[:], bch(stat[:, 4:8], 8), ALU.mult), sV)
                    op(VEC, lambda: v.tensor_reduce(
                        m4a, rt4[:], AX.X, ALU.max), sV)
                    op(VEC, lambda: v.tensor_tensor(
                        rwwb[:], rt4[:], bch(stat[:, 0:4], 8),
                        ALU.is_equal), sV)
                    op(VEC, lambda: v.tensor_scalar(
                        rt4b[:].rearrange("p a b -> p (a b)"),
                        rwwb[:].rearrange("p a b -> p (a b)"),
                        -1.0, 1.0, ALU.mult, ALU.add), sV)
                    op(VEC, lambda: v.tensor_mul(
                        rt4b[:].rearrange("p a b -> p (a b)"),
                        rt4[:].rearrange("p a b -> p (a b)"),
                        rt4b[:].rearrange("p a b -> p (a b)")), sV)
                    op(VEC, lambda: v.tensor_reduce(
                        m4b, rt4b[:], AX.X, ALU.max), sV)
                    op(VEC, lambda: v.tensor_tensor(
                        rt4b[:], rt4b[:], bch(stat[:, 4:8], 8),
                        ALU.is_equal), sV)
                    op(VEC, lambda: v.tensor_add(
                        rwwb[:].rearrange("p a b -> p (a b)"),
                        rwwb[:].rearrange("p a b -> p (a b)"),
                        rt4b[:].rearrange("p a b -> p (a b)")), sV)
                    op(VEC, lambda: v.tensor_mul(
                        rwwb[:].rearrange("p a b -> p (a b)"),
                        rt4[:].rearrange("p a b -> p (a b)"),
                        rwwb[:].rearrange("p a b -> p (a b)")), sV)
                    rww_v = sV.n
                    wait(SYN, sV, rww_v)
                    dma(SYN, rww_d[ti - 3:ti + 1].rearrange(
                        "t p e -> p t e"),
                        rwwb[:])

                if nstg < 4:
                    return
                wait(TEN, sV, dec_v)
                op(TEN, lambda pb=pb: t.matmul(
                    p4[0:64, 24:32], k_triu[0:64, :], dec_bf[0:64, pb, :],
                    start=True, stop=False, tile_position=(0, 0)), None)
                op(TEN, lambda pb=pb: t.matmul(
                    p4[0:64, 32:40], k_ustr[0:64, :], dec_bf[0:64, pb, :],
                    start=False, stop=True, tile_position=(0, 0)), None)
                op(TEN, lambda pb=pb: t.matmul(
                    p4[64:128, 24:32], k_triu[64:128, :], dec_bf[64:128, pb, :],
                    start=True, stop=False, tile_position=(64, 64)), None)
                op(TEN, lambda pb=pb: t.matmul(
                    p4[64:128, 32:40], k_ustr[64:128, :], dec_bf[64:128, pb, :],
                    start=False, stop=True, tile_position=(64, 64)), None)
                op(TEN, lambda pb=pb: t.matmul(
                    p4[:, 40:48], k_sum0[:], dec_bf[:, pb, :],
                    start=True, stop=False), None)
                op(TEN, lambda pb=pb: t.matmul(
                    p4[:, 48:56], k_sum1[:], dec_bf[:, pb, :],
                    start=False, stop=True), sT)
                g_t = sT.n
                wait(SCL, sT, g_t)
                wait(SCL, sV)
                wait(SCL, sGP)
                op(SCL, lambda: sc.activation(eg_b[:], p4[:, 24:32], AF.Exp,
                                              scale=-1.0), sSc)
                op(SCL, lambda: sc.activation(eglg_b[:], p4[:, 32:40],
                                              AF.Exp, scale=-1.0), sSc)
                op(SCL, lambda pb=pb: sc.activation(
                    eglr[:, pb].rearrange("p a b -> p (a b)"),
                    p4[:, 40:56], AF.Exp, scale=-1.0), sSc)
                eg_s = sSc.n
                fst[ti]["eg_s"] = eg_s

                if nstg < 5:
                    return
                wait(SCL, sT, proj_t)
                wait(SCL, sV)
                op(SCL, lambda: sc.activation(xsq[:, 0:512], p0[:],
                                              AF.Square), sSc)
                op(SCL, lambda: sc.activation(xctm[:, 0:512], p1[:],
                                              AF.Square), sSc)
                wait(VEC, sSc)
                op(VEC, lambda: v.tensor_reduce(
                    scl_b[:, 0:8].unsqueeze(2),
                    xsq[:, 0:512].rearrange("p (h k) -> p h k", k=64),
                    AX.X, ALU.add), sV)
                op(VEC, lambda: v.tensor_reduce(
                    scl_b[:, 8:16].unsqueeze(2),
                    xctm[:, 0:512].rearrange("p (h k) -> p h k", k=64),
                    AX.X, ALU.add), sV)
                wait(SCL, sV)
                op(SCL, lambda: sc.activation(stat[:, 8:16], scl_b[:, 0:8],
                                              AF.Sqrt, bias=k_eps[:, 0:1],
                                              scale=64.0), sSc)
                op(SCL, lambda: sc.activation(stat[:, 0:8], scl_b[:, 8:16],
                                              AF.Sqrt, bias=k_eps[:, 1:2]), sSc)
                wait(VEC, sSc)
                op(VEC, lambda: v.reciprocal(scl_b[:, 0:8], stat[:, 8:16]), sV)
                op(VEC, lambda: v.reciprocal(scl_b[:, 8:16], stat[:, 0:8]), sV)
                op(VEC, lambda pb=pb: v.tensor_tensor(
                    qn_b[:, pb].rearrange("p (h k) -> p h k", k=64),
                    p0[:].rearrange("p (h k) -> p h k", k=64),
                    bch(scl_b[:, 0:8], 64),
                    ALU.mult), sV)
                op(VEC, lambda pb=pb: v.tensor_tensor(
                    kn_b[:, pb].rearrange("p (h k) -> p h k", k=64),
                    p1[:].rearrange("p (h k) -> p h k", k=64),
                    bch(scl_b[:, 8:16], 64),
                    ALU.mult), sV)

                wait(VEC, sSc, eg_s)
                op(VEC, lambda ti=ti, pb=pb: v.tensor_scalar(
                    sm_t[:], beta_b[:, pb, :], rstdc[:, ti:ti + 1], None,
                    ALU.mult), sV)
                op(VEC, lambda pb=pb: v.tensor_tensor(
                    vb_b[:, pb].rearrange("p (h k) -> p h k", k=64),
                    p2[:].rearrange("p (h k) -> p h k", k=64),
                    bch(sm_t[:], 64), ALU.mult), sV)
                op(VEC, lambda ti=ti, pb=pb: v.tensor_scalar(
                    gt_b[:, pb], p3[:], rstdc[:, ti:ti + 1], None,
                    ALU.mult), sV)
                op(VEC, lambda pb=pb: v.tensor_tensor(
                    qe_b[:, pb].rearrange("p (h k) -> p h k", k=64),
                    qn_b[:, pb].rearrange("p (h k) -> p h k", k=64),
                    bch(eg_b[:], 64), ALU.mult), sV)
                op(VEC, lambda pb=pb: v.tensor_tensor(
                    ke2[:, pb, :, 0:64],
                    kn_b[:, pb].rearrange("p (h k) -> p h k", k=64),
                    bch(eglg_b[:], 64), ALU.mult), sV)
                wait(GPS, sV)
                wait(GPS, sSc)
                op(GPS, lambda pb=pb: gp.tensor_tensor(
                    ke2[:, pb, :, 64:128],
                    kn_b[:, pb].rearrange("p (h k) -> p h k", k=64),
                    bch(eglg_b[:], 64), ALU.mult), sGP)
                wait(VEC, sGP)
                op(VEC, lambda pb=pb: v.tensor_mul(sm_u[:], beta_b[:, pb, :],
                                                   eg_b[:]), sV)
                wait(GPS, sV)
                op(GPS, lambda pb=pb: gp.tensor_tensor(
                    kbg_b[:, pb].rearrange("p (h k) -> p h k", k=64),
                    kn_b[:, pb].rearrange("p (h k) -> p h k", k=64),
                    bch(sm_u[:], 64), ALU.mult), sGP)
                fst[ti]["scal_v"] = sV.n
                fst[ti]["scal_g"] = sGP.n

            def back(ti):
                pb = ti % 2
                i4 = ti % 4
                if nstg < 6 or "scal_v" not in fst[ti]:
                    return
                qn = qn_b[:, pb]
                kn = kn_b[:, pb]
                qe = qe_b[:, pb]
                vb = vb_b[:, pb]
                kbg = kbg_b[:, pb]
                wait(TEN, sV, fst[ti]["scal_v"])
                wait(TEN, sSc)
                for (srcb, dstb) in ((qn, qnK), (qe, qeK), (kn, knK)):
                    wait(TEN, sV)
                    for hp in range(4):
                        op(TEN, lambda srcb=srcb, hp=hp: t.transpose(
                            p6b[:, hp * 128:(hp + 1) * 128],
                            srcb[:, hp * 128:(hp + 1) * 128], k_i128[:]),
                           sT if hp == 3 else None)
                    tp = sT.n
                    wait(VEC, sT, tp)
                    op(VEC, lambda dstb=dstb: v.tensor_copy(
                        dstb[:].rearrange("p a b -> p (a b)"),
                        p6b[:, 0:512]), sV)
                km_v = sV.n
                for (srcK, dstK) in ((qnK, qnKd), (qeK, qeKd), (knK, knKd)):
                    wait(TEN, sV)
                    wait(TEN, sSc)
                    op(TEN, lambda srcK=srcK: t.matmul(
                        p6[:, 0:512], k_swp[:],
                        srcK[:].rearrange("p a b -> p (a b)"),
                        start=True, stop=True), sT)
                    dp = sT.n
                    wait(SCL, sT, dp)
                    op(SCL, lambda dstK=dstK: sc.copy(
                        dstK[:].rearrange("p a b -> p (a b)"),
                        p6[:, 0:512]), sSc)
                kd_s = sSc.n

                if nstg < 7:
                    return
                op(VEC, lambda pb=pb: v.tensor_tensor(
                    rhsU[:].rearrange("p (h m) -> p h m", m=64),
                    bck(k_ustr[:], 8),
                    bch(dec_b[:, pb, :], 64), ALU.mult), sV)
                wait(GPS, sV)
                op(GPS, lambda pb=pb: gp.tensor_tensor(
                    rhsL[:].rearrange("p (h m) -> p h m", m=64),
                    bck(k_triu[:], 8),
                    bch(dec_b[:, pb, :], 64), ALU.mult), sGP)
                lb_v = sV.n
                lb_g = sGP.n
                wait(TEN, sV, lb_v)
                wait(TEN, sGP, lb_g)
                for sp in range(2):
                    r0 = sp * 64
                    op(TEN, lambda r0=r0: t.matmul(
                        p5[r0:r0 + 64, :], k_triu[r0:r0 + 64, :],
                        rhsU[r0:r0 + 64, :], start=True,
                        stop=True, tile_position=(r0, r0)), None)
                    op(TEN, lambda r0=r0: t.matmul(
                        p7[r0:r0 + 64, :], k_ustr[r0:r0 + 64, :],
                        rhsL[r0:r0 + 64, :], start=True,
                        stop=True, tile_position=(r0, r0)),
                       sT if r0 == 64 else None)
                d_t = sT.n
                wait(SCL, sT, d_t)
                wait(SCL, sV)
                op(SCL, lambda: sc.activation(Lst[:], p5[:], AF.Exp,
                                              scale=-1.0), sSc)
                op(SCL, lambda: sc.activation(LTm[:], p7[:], AF.Exp,
                                              scale=-1.0), sSc)
                exp_s = sSc.n
                wait(VEC, sSc, exp_s)
                op(VEC, lambda: v.tensor_tensor(
                    Lst[:].rearrange("p (h m) -> p h m", m=64),
                    Lst[:].rearrange("p (h m) -> p h m", m=64),
                    bck(k_strict[:], 8), ALU.mult), sV)
                wait(GPS, sSc, exp_s)
                op(GPS, lambda: gp.tensor_tensor(
                    LTm[:].rearrange("p (h m) -> p h m", m=64),
                    LTm[:].rearrange("p (h m) -> p h m", m=64),
                    bck(k_trilT[:], 8), ALU.mult), sGP)
                lm_v = sV.n
                lm_g = sGP.n

                if nstg < 8:
                    return
                wait(TEN, sV, max(km_v, lm_v))
                wait(TEN, sGP, lm_g)
                wait(TEN, sSc, kd_s)
                for h in range(8):
                    hp = h // 2
                    for sp in range(2):
                        r0 = sp * 64
                        knS = knK if (h % 2) == sp else knKd
                        qnS = qnK if (h % 2) == sp else qnKd
                        op(TEN, lambda knS=knS, hp=hp, sp=sp, r0=r0, h=h:
                           t.matmul(
                            p5[r0:r0 + 64, h * 64:(h + 1) * 64],
                            knS[r0:r0 + 64, hp, sp * 64:(sp + 1) * 64],
                            knS[r0:r0 + 64, hp, sp * 64:(sp + 1) * 64],
                            start=(h == 0), stop=(h == 7),
                            tile_position=(r0, r0)), None)
                        op(TEN, lambda knS=knS, qnS=qnS, hp=hp, sp=sp,
                           r0=r0, h=h: t.matmul(
                            p7[r0:r0 + 64, h * 64:(h + 1) * 64],
                            knS[r0:r0 + 64, hp, sp * 64:(sp + 1) * 64],
                            qnS[r0:r0 + 64, hp, sp * 64:(sp + 1) * 64],
                            start=(h == 0), stop=(h == 7),
                            tile_position=(r0, r0)),
                           sT if (h == 7 and sp == 1) else None)
                ga_t = sT.n
                wait(VEC, sT, ga_t)
                op(VEC, lambda pb=pb: v.tensor_tensor(
                    Pk_s[:].rearrange("p (h m) -> p h m", m=64),
                    p5[:].rearrange("p (h m) -> p h m", m=64),
                    bch(beta_b[:, pb, :], 64), ALU.mult), sV)
                op(VEC, lambda: v.tensor_mul(Pk_s[:], Pk_s[:], Lst[:]), sV)
                wait(VEC, sGP, lm_g)
                op(VEC, lambda: v.tensor_mul(attT[:], p7[:], LTm[:]), sV)
                pb_v = sV.n

                if nstg < 9:
                    return
                wait(TEN, sV, pb_v)
                for h in range(8):
                    for sp in range(2):
                        r0 = sp * 64
                        op(TEN, lambda h=h, r0=r0: t.transpose(
                            p6b[r0:r0 + 64, h * 64:(h + 1) * 64],
                            Pk_s[r0:r0 + 64, h * 64:(h + 1) * 64],
                            k_i128[r0:r0 + 64, r0:r0 + 64],
                            tile_position=(r0, r0)),
                           sT if (h == 7 and r0 == 64) else None)
                it_t = sT.n
                wait(VEC, sT, it_t)
                op(VEC, lambda: v.tensor_copy(PkT_s[:], p6b[:, 0:512]), sV)
                op(VEC, lambda: v.scalar_tensor_tensor(
                    Av_s[:].rearrange("p (h m) -> p h m", m=64),
                    Pk_s[:].rearrange("p (h m) -> p h m", m=64),
                    1.0, bck(k_i64s[:], 8), ALU.mult, ALU.add), sV)
                ser_v = sV.n
                ser_c = sSc.n
                for it in range(6):
                    wait(TEN, sV, ser_v)
                    wait(TEN, sSc, ser_c)
                    for h in range(8):
                        hs = slice(h * 64, (h + 1) * 64)
                        for sp in range(2):
                            r0 = sp * 64
                            first = (h == 0)
                            last = (h == 7)
                            if it < 5:
                                op(TEN, lambda hs=hs, r0=r0, first=first,
                                   last=last: t.matmul(
                                    p5[r0:r0 + 64, hs], PkT_s[r0:r0 + 64, hs],
                                    Pk_s[r0:r0 + 64, hs], start=first,
                                    stop=last, tile_position=(r0, r0)), None)
                                op(TEN, lambda hs=hs, r0=r0, first=first,
                                   last=last: t.matmul(
                                    p7[r0:r0 + 64, hs], Pk_s[r0:r0 + 64, hs],
                                    PkT_s[r0:r0 + 64, hs], start=first,
                                    stop=last,
                                    tile_position=(r0, r0)),
                                   sT if (it == 0 and last and sp == 1)
                                   else None)
                            if it >= 1:
                                op(TEN, lambda hs=hs, r0=r0, first=first:
                                   t.matmul(
                                    p6[r0:r0 + 64, hs],
                                    k_i128[r0:r0 + 64, r0:r0 + 64],
                                    Av_s[r0:r0 + 64, hs], start=first,
                                    stop=False, tile_position=(r0, r0)), None)
                                op(TEN, lambda hs=hs, r0=r0,
                                   last=last: t.matmul(
                                    p6[r0:r0 + 64, hs], PkT_s[r0:r0 + 64, hs],
                                    Av_s[r0:r0 + 64, hs], start=False,
                                    stop=last, tile_position=(r0, r0)),
                                   sT if (last and sp == 1) else None)
                    si_t = sT.n
                    wait(VEC, sT, si_t)
                    wait(SCL, sT, si_t)
                    if it < 5:
                        op(VEC, lambda: v.tensor_copy(Pk_s[:], p5[:]), sV)
                        op(SCL, lambda: sc.copy(PkT_s[:], p7[:]), sSc)
                    if it >= 1:
                        op(SCL, lambda: sc.copy(Av_s[:], p6[:, 0:512]), sSc)
                    ser_v = sV.n
                    ser_c = sSc.n
                    wait(TEN, sSc, ser_c)
                wait(TEN, sV, ser_v)
                wait(TEN, sSc, ser_c)
                for h in range(8):
                    for sp in range(2):
                        r0 = sp * 64
                        op(TEN, lambda h=h, r0=r0: t.transpose(
                            p6b[r0:r0 + 64, h * 64:(h + 1) * 64],
                            Av_s[r0:r0 + 64, h * 64:(h + 1) * 64],
                            k_i128[r0:r0 + 64, r0:r0 + 64],
                            tile_position=(r0, r0)),
                           sT if (h == 7 and r0 == 64) else None)
                at_t = sT.n
                wait(VEC, sT, at_t)
                op(VEC, lambda: v.tensor_copy(AvT_s[:], p6b[:, 0:512]), sV)
                avt_v = sV.n

                if nstg < 10:
                    return
                wait(TEN, sV, avt_v)
                wait(TEN, sGP, fst[ti]["scal_g"])
                for h in range(8):
                    hs = slice(h * 64, (h + 1) * 64)
                    for sp in range(2):
                        r0 = sp * 64
                        op(TEN, lambda r0=r0, hs=hs, h=h, vb=vb: t.matmul(
                            p5[r0:r0 + 64, hs], AvT_s[r0:r0 + 64, hs],
                            vb[r0:r0 + 64, hs],
                            start=(h == 0), stop=(h == 7),
                            tile_position=(r0, r0)), None)
                        op(TEN, lambda r0=r0, hs=hs, h=h, kbg=kbg: t.matmul(
                            p7[r0:r0 + 64, hs],
                            kbg[r0:r0 + 64, hs], AvT_s[r0:r0 + 64, hs],
                            start=(h == 0), stop=(h == 7),
                            tile_position=(r0, r0)),
                           sT if (h == 7 and sp == 1) else None)
                uw_t = sT.n
                wait(VEC, sT, uw_t)
                op(VEC, lambda: v.tensor_copy(u_s[:], p5[:]), sV)
                op(VEC, lambda: v.tensor_copy(w_s[:], p7[:]), sV)
                uw_v = sV.n

                if nstg < 11:
                    return
                wait(SCL, sV)
                op(SCL, lambda pb=pb: sc.activation(xsq2[:], gt_b[:, pb],
                                                    AF.Silu), sSc)
                wait(TEN, sGP, fst[ti]["scal_g"])
                wait(VEC, sSc, fst[ti]["eg_s"])
                for sp in range(2):
                    r0 = sp * 64
                    vn = vn0 if sp == 0 else vn1
                    wait(TEN, sV)
                    for h in range(8):
                        hs = slice(h * 64, (h + 1) * 64)
                        op(TEN, lambda hs=hs, r0=r0, h=h: t.matmul(
                            p6[r0:r0 + 64, hs], w_s[r0:r0 + 64, hs],
                            S_bf[r0:r0 + 64, hs],
                            start=(h == 0), stop=(h == 7),
                            tile_position=(r0, r0)),
                           sT if h == 7 else None)
                    vn_t = sT.n
                    wait(VEC, sT, vn_t)
                    op(VEC, lambda r0=r0, vn=vn: v.scalar_tensor_tensor(
                        vn[r0:r0 + 64, :], p6[r0:r0 + 64, 0:512], -1.0,
                        u_s[r0:r0 + 64, :], ALU.mult, ALU.add), sV)
                    vn_v = sV.n
                    wait(TEN, sV, vn_v)
                    for h in range(8):
                        hp = h // 2
                        hs = slice(h * 64, (h + 1) * 64)
                        qeS = qeK if (h % 2) == sp else qeKd
                        op(TEN, lambda qeS=qeS, hs=hs, hp=hp, sp=sp, r0=r0,
                           h=h: t.matmul(
                            p7[r0:r0 + 64, hs],
                            qeS[r0:r0 + 64, hp, sp * 64:(sp + 1) * 64],
                            S_bf[r0:r0 + 64, hs],
                            start=(h == 0), stop=False,
                            tile_position=(r0, r0)), None)
                        op(TEN, lambda hs=hs, r0=r0, h=h, vn=vn: t.matmul(
                            p7[r0:r0 + 64, hs], attT[r0:r0 + 64, hs],
                            vn[r0:r0 + 64, hs], start=False,
                            stop=(h == 7), tile_position=(r0, r0)), None)
                        op(TEN, lambda hs=hs, h=h, vn=vn, pb=pb: t.matmul(
                            p5[:, hs], ke2[:, pb, h, :], vn[:, hs],
                            start=(h == 0), stop=(h == 7)),
                           sT if h == 7 else None)
                    su_t = sT.n
                    wait(SCL, sT, su_t)
                    op(SCL, lambda r0=r0: sc.copy(
                        o_b[r0:r0 + 64, :], p7[r0:r0 + 64, :]), sSc)
                    wait(VEC, sT, su_t)
                    op(VEC, lambda sp=sp, pb=pb: v.tensor_tensor(
                        Stmp[:].rearrange("p (a b) -> p a b", b=64),
                        S_st[:].rearrange("p (a b) -> p a b", b=64),
                        bch(eglr[:, pb, sp, :], 64),
                        ALU.mult), sV)
                    op(VEC, lambda: v.tensor_add(S_st[:], Stmp[:],
                                                 p5[:, 0:512]), sV)
                    wait(SCL, sV)
                    op(SCL, lambda: sc.copy(S_bf[:], S_st[:]), sSc)
                    wait(TEN, sSc)

                if nstg < 12:
                    return
                wait(SCL, sV)
                op(SCL, lambda: sc.activation(xsq[:, 0:512], o_b[:],
                                              AF.Square), sSc)
                wait(VEC, sSc)
                op(VEC, lambda: v.tensor_reduce(
                    ms_b[:].unsqueeze(2),
                    xsq[:, 0:512].rearrange("p (h k) -> p h k", k=64),
                    AX.X, ALU.add), sV)
                wait(SCL, sV)
                op(SCL, lambda: sc.activation(ms_b[:], ms_b[:], AF.Sqrt,
                                              bias=k_eps[:, 4:5],
                                              scale=1.0 / 64), sSc)
                wait(VEC, sSc)
                op(VEC, lambda: v.reciprocal(ms_b[:], ms_b[:]), sV)
                op(VEC, lambda: v.tensor_mul(og_b[:], o_b[:],
                                             xsq2[:]), sV)
                op(VEC, lambda: v.tensor_tensor(
                    og_b[:].rearrange("p (h k) -> p h k", k=64),
                    og_b[:].rearrange("p (h k) -> p h k", k=64),
                    bch(ms_b[:], 64), ALU.mult), sV)
                wait(GPS, sV)
                op(GPS, lambda: gp.tensor_tensor(
                    og_b[:].rearrange("p (h k) -> p h k", k=64),
                    og_b[:].rearrange("p (h k) -> p h k", k=64),
                    bck(k_nw[:], 8), ALU.mult), sGP)
                og_v = sV.n
                og_g = sGP.n
                wait(TEN, sV, og_v)
                wait(TEN, sGP, og_g)
                for hp in range(4):
                    op(TEN, lambda hp=hp: t.transpose(
                        p6b[:, hp * 128:(hp + 1) * 128],
                        og_b[:, hp * 128:(hp + 1) * 128], k_i128[:]),
                       sT if hp == 3 else None)
                ogt_t = sT.n
                wait(SCL, sT, ogt_t)
                wait(SCL, sLD)
                op(SCL, lambda i4=i4: sc.copy(
                    ogf_b[:, :, i4 * 128:(i4 + 1) * 128],
                    p6b[:, 0:512].rearrange("p (a b) -> p a b", b=128)), sSc)
                if i4 == 3:
                    t0 = (ti // 4) * 512
                    ogf_v2 = sSc.n
                    wait(SYN, sSc, ogf_v2)
                    for hp in range(4):
                        dma(SYN, ogall[hp * 128:(hp + 1) * 128, t0:t0 + 512],
                            ogf_b[:, hp, :])

            NTI = min(NT, 4 * nblk) if nblk < NT else NT
            wait(GPS, sV)
            wait(GPS, sSc)
            dma(GPS, xtm[:, 0, :], xTM[0:128, :])
            for ti in range(NTI):
                fst[ti] = {"xl": sGD.n}
                front(ti)
                if ti >= 1:
                    back(ti - 1)
                    del fst[ti - 1]
            if NTI >= 1:
                back(NTI - 1)
            if phases < 2:
                op(VEC, lambda: v.memset(o_t1[:], 0.0), sV)
                wait(SYN, sV)
                for ob in range(NOB):
                    for tt in range(4):
                        dma(SYN, outY[ob * 512 + tt * 128: ob * 512 + (tt + 1) * 128, :],
                            o_t1[:], sST)
                wait(SYN, sST)
                return
            wait(GPS, sLD)
            rank = None
            if me == GPS:
                rank = gp.cc_rank(PAIR_GROUPS)

            def ccin_dma():
                src = ogall[:, 0:OWNT]
                src = dataclasses.replace(src, offset=(1 - rank) * OWNT)
                return gp.dma_start(out=cc_in.rearrange("(a b) -> a b", a=512),
                                    in_=src)
            dmaop(GPS, ccin_dma)
            wait(GPS, sGD)
            op(GPS, lambda: gp.collective_compute(
                "AllGather", ALU.bypass, replica_groups=PAIR_GROUPS,
                ins=[cc_in], outs=[cc_out]), sCC)
            wait(GPS, sCC)
            wait(SYN, sCC)
            wait(TEN, sCC)

            rank_s = None
            if me == SYN:
                rank_s = sy.cc_rank(PAIR_GROUPS)

            def cp_ogown():
                src = dataclasses.replace(ogall[:, 0:OWNT], offset=rank_s * OWNT)
                return sy.dma_start(out=og_own, in_=src)
            dmaop(SYN, cp_ogown)

            def cp_ogpar():
                src = dataclasses.replace(cc_out[0:512 * OWNT],
                                          offset=(1 - rank_s) * (512 * OWNT))
                return sy.dma_start(out=og_par.rearrange("a b -> (a b)"), in_=src)
            dmaop(SYN, cp_ogpar)

            def cp_xcs():
                src = dataclasses.replace(xcs[:, :, 0:OWNT], offset=rank_s * OWNT)
                return sy.dma_start(out=xcs_own, in_=src)
            dmaop(SYN, cp_xcs)

            def cp_rww():
                src = dataclasses.replace(
                    rww_d[0:NT // 2], offset=rank_s * (NT // 2) * (128 * 8))
                return sy.dma_start(out=rww_own, in_=src)
            dmaop(SYN, cp_rww)
            wait(SYN, sLD)

            if phases < 3:
                op(VEC, lambda: v.memset(o_t1[:], 0.0), sV)
                wait(SYN, sV)
                for ob in range(NOB):
                    for tt in range(4):
                        dma(SYN, outY[ob * 512 + tt * 128: ob * 512 + (tt + 1) * 128, :],
                            o_t1[:], sST)
                wait(SYN, sST)
                return
            for dc in range(8):
                dma(GPS, W_e[:, 0, dc, :], wo[dc * 128:(dc + 1) * 128, :])
            wo_g = sGD.n
            wait(TEN, sGD, wo_g)
            for ob in range(NOB):
                wait(SYN, sT)
                for hp in range(4):
                    dma(SYN, xcb[:, hp, :],
                        og_own[hp * 128:(hp + 1) * 128, ob * 512:(ob + 1) * 512])
                    dma(SYN, xcb[:, 4 + hp, :],
                        og_par[hp * 128:(hp + 1) * 128, ob * 512:(ob + 1) * 512])
                ogl = sLD.n
                wait(TEN, sLD, ogl)
                for tt in range(4):
                    ot = ob * 4 + tt
                    wait(TEN, sV)
                    for dc in range(8):
                        for jh in range(2):
                            op(TEN, lambda tt=tt, jh=jh, dc=dc: t.matmul(
                                (p0 if jh == 0 else p1)[:],
                                xcb[:, dc, tt * 128:(tt + 1) * 128],
                                W_e[:, 0, dc, jh * 512:(jh + 1) * 512],
                                start=(dc == 0), stop=(dc == 7)),
                               sT if (dc == 7 and jh == 1) else None)
                    wo_t = sT.n
                    wait(VEC, sT, wo_t)
                    wait(VEC, sST)
                    op(VEC, lambda: v.tensor_copy(og_b[:], p0[:]), sV)
                    op(VEC, lambda: v.tensor_copy(o_b[:], p1[:]), sV)
                    at_v = sV.n
                    wait(SYN, sV, at_v)
                    dma(SYN, attn_d[ot, :, 0:512], og_b[:], sST)
                    dma(SYN, attn_d[ot, :, 512:1024], o_b[:], sST)

            if phases < 4:
                op(VEC, lambda: v.memset(o_t1[:], 0.0), sV)
                wait(SYN, sV)
                for ob in range(NOB):
                    for tt in range(4):
                        dma(SYN, outY[ob * 512 + tt * 128: ob * 512 + (tt + 1) * 128, :],
                            o_t1[:], sST)
                wait(SYN, sST)
                return
            prevmm = sT.n
            wuser = [prevmm, prevmm]
            for ob in range(NOB):
                wait(SYN, sT, prevmm)
                for dc in range(8):
                    dma(SYN, xcb[:, dc, :], xcs_own[:, dc, ob * 512:(ob + 1) * 512])
                for tt in range(4):
                    dma(SYN, rwwb[:, tt, :], rww_own[ob * 4 + tt])
                xcl = sLD.n
                wait(VEC, sLD, xcl)
                op(VEC, lambda: v.tensor_copy(
                    rwwb_bf[:].rearrange("p a b -> p (a b)"),
                    rwwb[:].rearrange("p a b -> p (a b)")), sV)
                ev_half = [sV.n, sV.n]
                mmdone = []
                for e in range(8):
                    wait(GPS, sT, mmdone[e - 2] if e >= 2 else wuser[e % 2])
                    for dc in range(8):
                        def ld_we(e=e, dc=dc):
                            return gp.dma_start(
                                out=W_e[:, e % 2, dc, :],
                                in_=weT[e, dc * 128:(dc + 1) * 128, :])
                        dmaop(GPS, ld_we)
                    wsm_n = sGD.n
                    wait(TEN, sGD, wsm_n)
                    if e == 0:
                        wait(TEN, sLD, xcl)
                    for th in range(2):
                        wait(TEN, sV, ev_half[th])
                        for tt in (th * 2, th * 2 + 1):
                            for dc in range(8):
                                for jh in range(2):
                                    bank = banks[tt * 2 + jh]
                                    op(TEN, lambda bank=bank, tt=tt, dc=dc,
                                       jh=jh, e=e: t.matmul(
                                        bank[:],
                                        xcb[:, dc, tt * 128:(tt + 1) * 128],
                                        W_e[:, e % 2, dc, jh * 512:(jh + 1) * 512],
                                        start=(dc == 0), stop=(dc == 7)),
                                       sT if (tt == th * 2 + 1 and dc == 7
                                              and jh == 1) else None)
                        mm_half = sT.n
                        wait(VEC, sT, mm_half)
                        for tt in (th * 2, th * 2 + 1):
                            for jh in range(2):
                                bank = banks[tt * 2 + jh]
                                dst = ffa[:, tt, jh * 512:(jh + 1) * 512]
                                if e == 0:
                                    op(VEC, lambda bank=bank, dst=dst, tt=tt,
                                       e=e: v.tensor_scalar(
                                        dst, bank[:], rwwb[:, tt, e:e + 1],
                                        None, ALU.mult), sV)
                                else:
                                    op(VEC, lambda bank=bank, dst=dst, tt=tt,
                                       e=e: v.scalar_tensor_tensor(
                                        dst, bank[:], rwwb[:, tt, e:e + 1],
                                        dst, ALU.mult, ALU.add), sV)
                        ev_half[th] = sV.n
                    mmdone.append(sT.n)
                wuser = [mmdone[6], mmdone[7]]
                for tt in range(4):
                    ot = ob * 4 + tt

                    def ld_at(ot=ot):
                        return sy.dma_start(out=atile[:], in_=attn_d[ot])
                    wait(SYN, sV)
                    dmaop(SYN, ld_at)
                    dma(SYN, xtf[:],
                        xOWN[ob * 512 + tt * 128: ob * 512 + (tt + 1) * 128, :])
                    xfl = sLD.n
                    wait(TEN, sV, ev_half[1])
                    op(TEN, lambda tt=tt: t.transpose(
                        p6b[0:8, 0:128], rwwb_bf[:, tt, :], k_i128[:]), sT)
                    rwt_t = sT.n
                    wait(SCL, sT, rwt_t)
                    op(SCL, lambda: sc.copy(rwT_ob[0:8, :],
                                            p6b[0:8, 0:128]), sSc)
                    rwg = sSc.n
                    wait(TEN, sSc, rwg)
                    wait(TEN, sV, ev_half[tt // 2])
                    for jh in range(2):
                        op(TEN, lambda tt=tt, jh=jh: t.matmul(
                            banks[tt * 2 + jh][:], rwT_ob[0:8, :],
                            B_e[:, jh * 512:(jh + 1) * 512], start=True,
                            stop=True), sT if jh == 1 else None)
                    fb_t = sT.n
                    wait(VEC, sT, fb_t)
                    wait(VEC, sLD, xfl)
                    wait(VEC, sST)
                    for jh in range(2):
                        op(VEC, lambda jh=jh, tt=tt: v.tensor_add(
                            o_t1[:, jh * 512:(jh + 1) * 512],
                            ffa[:, tt, jh * 512:(jh + 1) * 512],
                            banks[tt * 2 + jh][:]), sV)
                    op(VEC, lambda tt=tt: v.tensor_add(
                        o_t1[:].rearrange("p (a b) -> p a b", a=2),
                        o_t1[:].rearrange("p (a b) -> p a b", a=2),
                        atile[:].rearrange("p (a b) -> p a b", a=2)), sV)
                    op(VEC, lambda: v.tensor_add(o_t1[:], o_t1[:], xtf[:]), sV)
                    out_v = sV.n
                    wait(SYN, sV, out_v)
                    dma(SYN, outY[ob * 512 + tt * 128: ob * 512 + (tt + 1) * 128, :],
                        o_t1[:], sST)
                prevmm = sT.n
            wait(SYN, sST)

        @block.sync
        def _(_e):
            program("sync")

        @block.gpsimd
        def _(_e):
            program("gpsimd")

        @block.vector
        def _(_e):
            program("vector")

        @block.scalar
        def _(_e):
            program("scalar")

        @block.tensor
        def _(_e):
            program("tensor")

    return nc


def _consts():
    tri = np.triu(np.ones((C, C), np.float32))
    ust = np.tril(np.ones((C, C), np.float32), -1)
    stc = np.tril(np.ones((C, C), np.float32), -1)
    tlt = np.triu(np.ones((C, C), np.float32))
    bf = ml_dtypes.bfloat16
    return {
        "c_triu": np.tile(tri, (2, 1)).astype(bf),
        "c_ustr": np.tile(ust, (2, 1)).astype(bf),
        "c_strict": np.tile(stc, (2, 1)).astype(bf),
        "c_trilT": np.tile(tlt, (2, 1)).astype(bf),
        "csum0": np.vstack([np.ones((64, 128)), np.zeros((64, 128))]).astype(bf),
        "csum1": np.vstack([np.zeros((64, 128)), np.ones((64, 128))]).astype(bf),
        "cswp": np.roll(np.eye(128), 64, axis=0).astype(bf),
        "c_i128": np.eye(128).astype(bf),
        "c_i64s": np.tile(np.eye(64, dtype=np.float32), (2, 1)),
    }


def kernel_impl(inputs, n_cores, runner):
    x = np.asarray(inputs["L0_attn_x"], np.float32)
    Bb, T, _ = x.shape
    Wq = np.asarray(inputs["L0_attn_Wq"], np.float32)
    Wk = np.asarray(inputs["L0_attn_Wk"], np.float32)
    Wv = np.asarray(inputs["L0_attn_Wv"], np.float32)
    Wg = np.asarray(inputs["L0_attn_Wg"], np.float32)
    Wo = np.asarray(inputs["L0_attn_Wo"], np.float32)
    Wb = np.asarray(inputs["L0_attn_Wb"], np.float32)
    Wa = np.asarray(inputs["L0_attn_Wa"], np.float32)
    dt_bias = np.asarray(inputs["L0_attn_dt_bias"], np.float32)
    A_log = np.asarray(inputs["L0_attn_A_log"], np.float32)
    norm_w = np.asarray(inputs["L0_attn_norm_w"], np.float32)
    We = np.asarray(inputs["L0_ffn_W_experts"], np.float32)
    be = np.asarray(inputs["L0_ffn_b_experts"], np.float32)
    Wr = np.asarray(inputs["L0_ffn_W_router"], np.float32)
    br = np.asarray(inputs["L0_ffn_b_router"], np.float32)

    consts = _consts()
    weT = np.ascontiguousarray(We.transpose(0, 2, 1))
    in_maps = []
    for cidx in range(n_cores):
        b, g = cidx // 2, cidx % 2
        hs = slice(g * 512, (g + 1) * 512)
        h8 = slice(g * 8, (g + 1) * 8)
        im = {
            "xTM": np.ascontiguousarray(x[b]),
            "wq": np.ascontiguousarray(Wq[:, hs]),
            "wk": np.ascontiguousarray(Wk[:, hs]),
            "wv": np.ascontiguousarray(Wv[:, hs]),
            "wg": np.ascontiguousarray(Wg[:, hs]),
            "wsm": np.ascontiguousarray(np.concatenate(
                [Wb[:, h8], Wa[:, h8], Wr.T], axis=1)),
            "wo": np.ascontiguousarray(np.concatenate(
                [Wo[g * 512:(g + 1) * 512], Wo[(1 - g) * 512:(2 - g) * 512]],
                axis=0)),
            "weT": weT,
            "bexp": be,
            "dtb": np.ascontiguousarray(np.broadcast_to(dt_bias[h8], (128, 8))),
            "alog": np.ascontiguousarray(np.broadcast_to(A_log[h8], (128, 8))),
            "brr": np.ascontiguousarray(np.broadcast_to(br, (128, 8))),
            "nwr": np.ascontiguousarray(np.broadcast_to(norm_w, (128, 64))),
        }
        im["epsc"] = np.ascontiguousarray(np.broadcast_to(
            np.array([64e-6, 1e-6, 1.0, 0.0, 1e-5, 0.0], np.float32),
            (128, 6)))
        im["xOWN"] = np.ascontiguousarray(x[b, g * (T // 2):(g + 1) * (T // 2)])
        im.update(consts)
        in_maps.append(im)

    nc = build_nc(T=T, n_cores=n_cores)
    res = runner(nc, in_maps)
    out = np.zeros((Bb, T, D), np.float32)
    half = T // 2
    for cidx in range(n_cores):
        b, g = cidx // 2, cidx % 2
        out[b, g * half:(g + 1) * half] = res[cidx]["outY"]
    return out


def kernel(**inputs):
    def runner(nc, in_maps):
        r = run_bass_kernel_spmd(nc, in_maps, core_ids=list(range(len(in_maps))))
        return r.results
    return kernel_impl(inputs, 8, runner)


# version 15
# speedup vs baseline: 1.0140x; 1.0140x over previous
import dataclasses
import numpy as np
import ml_dtypes

import concourse.bass as bass
import concourse.mybir as mybir
from concourse.bass_utils import run_bass_kernel_spmd

F32 = mybir.dt.float32
BF16 = mybir.dt.bfloat16
AF = mybir.ActivationFunctionType
ALU = mybir.AluOpType
AX = mybir.AxisListType

K, C, D = 64, 64, 1024

TEN, VEC, SCL, GPS, SYN = "tensor", "vector", "scalar", "gpsimd", "sync"


class Sem:
    def __init__(self, h):
        self.h = h
        self.n = 0


DMA_DEPTH = 4


def build_nc(T=8192, n_cores=8, phases=4, nblk=10**9, nstg=99):
    NT = T // 128
    NB = T // 512
    OWNT = T // 2
    NOB = OWNT // 512
    PAIR_GROUPS = [[i, i + 1] for i in range(0, n_cores, 2)]
    nc = bass.Bass("TRN2", target_bir_lowering=False, debug=False, num_devices=n_cores)

    xTM = nc.dram_tensor("xTM", [T, D], F32, kind="ExternalInput").ap()
    wq = nc.dram_tensor("wq", [D, 512], F32, kind="ExternalInput").ap()
    wk = nc.dram_tensor("wk", [D, 512], F32, kind="ExternalInput").ap()
    wv = nc.dram_tensor("wv", [D, 512], F32, kind="ExternalInput").ap()
    wg = nc.dram_tensor("wg", [D, 512], F32, kind="ExternalInput").ap()
    wsm = nc.dram_tensor("wsm", [D, 24], F32, kind="ExternalInput").ap()
    wo = nc.dram_tensor("wo", [D, D], F32, kind="ExternalInput").ap()
    weT = nc.dram_tensor("weT", [8, D, D], F32, kind="ExternalInput").ap()
    bexp = nc.dram_tensor("bexp", [8, D], F32, kind="ExternalInput").ap()
    dtb = nc.dram_tensor("dtb", [128, 8], F32, kind="ExternalInput").ap()
    alog = nc.dram_tensor("alog", [128, 8], F32, kind="ExternalInput").ap()
    brr = nc.dram_tensor("brr", [128, 8], F32, kind="ExternalInput").ap()
    nwr = nc.dram_tensor("nwr", [128, 64], F32, kind="ExternalInput").ap()
    c_triu = nc.dram_tensor("c_triu", [128, 64], BF16, kind="ExternalInput").ap()
    c_ustr = nc.dram_tensor("c_ustr", [128, 64], BF16, kind="ExternalInput").ap()
    c_strict = nc.dram_tensor("c_strict", [128, 64], BF16, kind="ExternalInput").ap()
    c_trilT = nc.dram_tensor("c_trilT", [128, 64], BF16, kind="ExternalInput").ap()
    csum0 = nc.dram_tensor("csum0", [128, 128], BF16, kind="ExternalInput").ap()
    csum1 = nc.dram_tensor("csum1", [128, 128], BF16, kind="ExternalInput").ap()
    cswp = nc.dram_tensor("cswp", [128, 128], BF16, kind="ExternalInput").ap()
    c_i128 = nc.dram_tensor("c_i128", [128, 128], BF16, kind="ExternalInput").ap()
    c_i64s = nc.dram_tensor("c_i64s", [128, 64], F32, kind="ExternalInput").ap()
    epsc = nc.dram_tensor("epsc", [128, 6], F32, kind="ExternalInput").ap()

    ogall = nc.dram_tensor("ogall", [512, T], BF16, kind="Internal").ap()
    xcs = nc.dram_tensor("xcs", [128, 8, T], BF16, kind="Internal").ap()
    rww_d = nc.dram_tensor("rww_d", [NT, 128, 8], F32, kind="Internal").ap()
    attn_d = nc.dram_tensor("attn_d", [OWNT // 128, 128, D], BF16, kind="Internal").ap()
    xOWN = nc.dram_tensor("xOWN", [OWNT, D], F32, kind="ExternalInput").ap()
    og_own = nc.dram_tensor("og_own", [512, OWNT], BF16, kind="Internal").ap()
    og_par = nc.dram_tensor("og_par", [512, OWNT], BF16, kind="Internal").ap()
    xcs_own = nc.dram_tensor("xcs_own", [128, 8, OWNT], BF16, kind="Internal").ap()
    rww_own = nc.dram_tensor("rww_own", [NT // 2, 128, 8], F32, kind="Internal").ap()
    cc_in = nc.dram_tensor("cc_in", [512 * OWNT], BF16, kind="Internal").ap()
    cc_out = nc.dram_tensor("cc_out", [2 * 512 * OWNT], BF16, kind="Internal").ap()
    outY = nc.dram_tensor("outY", [OWNT, D], F32, kind="ExternalOutput").ap()

    from contextlib import ExitStack
    es = ExitStack()
    SB = lambda name, shape, dt=F32: es.enter_context(nc.sbuf_tensor(name, shape, dt))
    PS = lambda name, shape, dt=F32: es.enter_context(nc.psum_tensor(name, shape, dt))
    SEM = lambda name: Sem(es.enter_context(nc.semaphore(name)))

    with es:
        k_triu = SB("k_triu", [128, 64], BF16)
        k_ustr = SB("k_ustr", [128, 64], BF16)
        k_strict = SB("k_strict", [128, 64], BF16)
        k_trilT = SB("k_trilT", [128, 64], BF16)
        k_sum0 = SB("k_sum0", [128, 128], BF16)
        k_sum1 = SB("k_sum1", [128, 128], BF16)
        k_swp = SB("k_swp", [128, 128], BF16)
        k_i128 = SB("k_i128", [128, 128], BF16)
        k_i64s = SB("k_i64s", [128, 64], F32)
        k_eps = SB("k_eps", [128, 6], F32)
        k_dtb = SB("k_dtb", [128, 8], F32)
        k_alog = SB("k_alog", [128, 8], F32)
        k_eA = SB("k_eA", [128, 8], F32)
        k_brr = SB("k_brr", [128, 8], F32)
        k_nw = SB("k_nw", [128, 64], F32)
        W_q = SB("W_q", [128, 8, 512], BF16)
        W_k = SB("W_k", [128, 8, 512], BF16)
        W_v = SB("W_v", [128, 8, 512], BF16)
        W_g = SB("W_g", [128, 8, 512], BF16)
        W_sm = SB("W_sm", [128, 8, 24], BF16)
        W_e = SB("W_e", [128, 2, 8, 1024], BF16)
        B_e = SB("B_e", [8, 1024], BF16)
        rstdc = SB("rstdc", [128, NT], F32)
        S_st = SB("S_st", [128, 512], F32)
        S_bf = SB("S_bf", [128, 512], BF16)
        Stmp = SB("Stmp", [128, 512], F32)
        eglr = SB("eglr", [128, 2, 2, 8], F32)
        xtm = SB("xtm", [128, 2, 1024], BF16)
        xsq = SB("xsq", [128, 1024], BF16)
        xsq2 = SB("xsq2", [128, 512], BF16)
        stat = SB("stat", [128, 16], F32)
        xcb = SB("xcb", [128, 8, 512], BF16)
        xctm = SB("xctm", [128, 1024], BF16)
        qn_b = SB("qn_b", [128, 2, 512], BF16)
        kn_b = SB("kn_b", [128, 2, 512], BF16)
        vb_b = SB("vb_b", [128, 2, 512], BF16)
        kbg_b = SB("kbg_b", [128, 2, 512], BF16)
        qe_b = SB("qe_b", [128, 2, 512], BF16)
        ke2 = SB("ke2", [128, 2, 8, 128], BF16)
        gt_b = SB("gt_b", [128, 2, 512], BF16)
        qnK = SB("qnK", [128, 4, 128], BF16)
        qeK = SB("qeK", [128, 4, 128], BF16)
        knK = SB("knK", [128, 4, 128], BF16)
        qnKd = SB("qnKd", [128, 4, 128], BF16)
        qeKd = SB("qeKd", [128, 4, 128], BF16)
        knKd = SB("knKd", [128, 4, 128], BF16)
        beta_b = SB("beta_b", [128, 2, 8], F32)
        dec_b = SB("dec_b", [128, 2, 8], F32)
        dec_bf = SB("dec_bf", [128, 2, 8], BF16)
        eg_b = SB("eg_b", [128, 8], F32)
        eglg_b = SB("eglg_b", [128, 8], F32)
        scl_b = SB("scl_b", [128, 16], F32)
        rhsU = SB("rhsU", [128, 512], BF16)
        rhsL = SB("rhsL", [128, 512], BF16)
        Lst = SB("Lst", [128, 512], BF16)
        LTm = SB("LTm", [128, 512], BF16)
        attT = SB("attT", [128, 512], BF16)
        Pk_s = SB("Pk_s", [128, 512], BF16)
        PkT_s = SB("PkT_s", [128, 512], BF16)
        Av_s = SB("Av_s", [128, 512], BF16)
        AvT_s = SB("AvT_s", [128, 512], BF16)
        u_s = SB("u_s", [128, 512], BF16)
        w_s = SB("w_s", [128, 512], BF16)
        vn0 = SB("vn0", [128, 512], BF16)
        vn1 = SB("vn1", [128, 512], BF16)
        o_b = SB("o_b", [128, 512], BF16)
        ms_b = SB("ms_b", [128, 8], F32)
        og_b = SB("og_b", [128, 512], BF16)
        atile = SB("atile", [128, 1024], BF16)
        ogf_b = SB("ogf_b", [128, 4, 512], BF16)
        xtf = SB("xtf", [128, 1024], F32)
        ffa = SB("ffa", [128, 4, 1024], F32)
        o_t1 = SB("o_t1", [128, 1024], F32)
        rww_t = SB("rww_t", [128, 8], F32)
        rwT_ob = SB("rwT_ob", [8, 128], BF16)
        rwwb_bf = SB("rwwb_bf", [128, 4, 8], BF16)
        sm_t = SB("sm_t", [128, 8], F32)
        rwwb = SB("rwwb", [128, 4, 8], F32)
        rt4 = SB("rt4", [128, 4, 8], F32)
        rt4b = SB("rt4b", [128, 4, 8], F32)
        sm_u = SB("sm_u", [128, 8], F32)
        sm_m = SB("sm_m", [128, 2], F32)

        p0 = PS("p0", [128, 512])
        p1 = PS("p1", [128, 512])
        p2 = PS("p2", [128, 512])
        p3 = PS("p3", [128, 512])
        p4 = PS("p4", [128, 512])
        p5 = PS("p5", [128, 512])
        p6 = PS("p6", [128, 512])
        p7 = PS("p7", [128, 512])
        banks = [p0, p1, p2, p3, p4, p5, p6, p7]

        sLD = SEM("sLD")
        sGD = SEM("sGD")
        sT = SEM("sT")
        sV = SEM("sV")
        sSc = SEM("sSc")
        sCC = SEM("sCC")
        sST = SEM("sST")
        sGP = SEM("sGP")

        block = es.enter_context(nc.Block())

        def program(eng):
            for _s in (sLD, sGD, sT, sV, sSc, sST, sCC, sGP):
                _s.n = 0
            E = {"sync": nc.sync, "gpsimd": nc.gpsimd, "vector": nc.vector,
                 "scalar": nc.scalar, "tensor": nc.tensor}[eng]
            me = eng
            v, sc, gp, sy = nc.vector, nc.scalar, nc.gpsimd, nc.sync

            class _T:
                @staticmethod
                def matmul(*a, **kw):
                    kw.setdefault("skip_group_check", True)
                    return nc.tensor.matmul(*a, **kw)

                @staticmethod
                def transpose(*a, **kw):
                    return nc.tensor.transpose(*a, **kw)
            t = _T()
            p6b = p6[:].bitcast(BF16)
            p4b = p4[:].bitcast(BF16)

            def dma(qeng, out, in_, sem=None, depth=None):
                depth = depth or DMA_DEPTH
                sem = sem or (sGD if qeng == GPS else sLD)
                if me == qeng:
                    E.wait_ge(sem.h, max(0, sem.n - 16 * (depth - 1)))
                    E.dma_start(out=out, in_=in_).then_inc(sem.h, 16)
                sem.n += 16

            def dmaop(qeng, fn, sem=None, depth=None):
                depth = depth or DMA_DEPTH
                sem = sem or (sGD if qeng == GPS else sLD)
                if me == qeng:
                    E.wait_ge(sem.h, max(0, sem.n - 16 * (depth - 1)))
                    fn().then_inc(sem.h, 16)
                sem.n += 16

            own_sem = {VEC: sV, SCL: sSc, GPS: sGP, TEN: sT}

            def op(oeng, fn, sem=None, k=1):
                if me == oeng:
                    if oeng in (VEC, SCL, GPS):
                        E.wait_ge(own_sem[oeng].h, own_sem[oeng].n)
                    ins = fn()
                    if sem is not None:
                        ins.then_inc(sem.h, k)
                if sem is not None:
                    sem.n += k

            def wait(weng, sem, val=None):
                if me == weng:
                    E.wait_ge(sem.h, sem.n if val is None else val)

            def bck(ap, nrep):
                return ap.unsqueeze(1).broadcast_to(
                    [ap.shape[0], nrep, ap.shape[1]])

            def bch(ap, rep):
                return ap.unsqueeze(2).broadcast_to(
                    [ap.shape[0], ap.shape[1], rep])

            for (dst, src) in [(k_triu, c_triu), (k_ustr, c_ustr),
                               (k_strict, c_strict), (k_trilT, c_trilT),
                               (k_sum0, csum0), (k_sum1, csum1),
                               (k_swp, cswp), (k_i128, c_i128),
                               (k_i64s, c_i64s), (k_eps, epsc), (k_dtb, dtb), (k_alog, alog),
                               (k_brr, brr), (k_nw, nwr)]:
                dma(SYN, dst[:], src)
            for (dst, src) in [(W_q, wq), (W_k, wk), (W_v, wv), (W_g, wg)]:
                for dc in range(8):
                    dma(GPS, dst[:, dc, :], src[dc * 128:(dc + 1) * 128, :])
            for dc in range(8):
                dma(GPS, W_sm[:, dc, :], wsm[dc * 128:(dc + 1) * 128, :])
            dma(GPS, B_e[:], bexp)
            wq_g = sGD.n
            wait(GPS, sGD, wq_g)
            kload = sLD.n
            wait(SCL, sLD, kload)
            op(SCL, lambda: sc.activation(k_eA[:], k_alog[:], AF.Exp), sSc)
            op(VEC, lambda: v.memset(S_st[:], 0.0), sV)
            op(VEC, lambda: v.memset(S_bf[:], 0.0), sV)
            op(VEC, lambda: v.memset(vn0[64:128, :], 0.0), sV)
            op(VEC, lambda: v.memset(vn1[0:64, :], 0.0), sV)

            fst = {}

            def front(ti):
                pb = ti % 2
                sl = ti % 2
                if ti + 1 < NTI:
                    wait(GPS, sV)
                    wait(GPS, sSc)
                    dma(GPS, xtm[:, (ti + 1) % 2, :],
                        xTM[(ti + 1) * 128: (ti + 2) * 128, :])
                xload = fst[ti]["xl"]
                wait(SCL, sV)
                wait(SCL, sGD, xload)
                op(SCL, lambda sl=sl: sc.activation(
                    xsq[:], xtm[:, sl, :], AF.Square,
                    accum_out=stat[:, 1:2]), sSc)
                wait(VEC, sGD, xload)
                op(VEC, lambda sl=sl: v.tensor_reduce(
                    stat[:, 0:1], xtm[:, sl, :], AX.X, ALU.add), sV)
                wait(VEC, sSc)
                op(VEC, lambda: v.tensor_scalar(stat[:, 2:3], stat[:, 0:1],
                                                1.0 / D, None, ALU.mult), sV)
                op(VEC, lambda: v.scalar_tensor_tensor(
                    stat[:, 3:4], stat[:, 2:3], 1.0, stat[:, 2:3],
                    ALU.mult, ALU.mult), sV)
                op(VEC, lambda: v.scalar_tensor_tensor(
                    stat[:, 5:6], stat[:, 1:2], 1.0 / D, stat[:, 3:4],
                    ALU.mult, ALU.subtract), sV)
                wait(SCL, sV)
                op(SCL, lambda: sc.activation(
                    stat[:, 6:7], stat[:, 5:6], AF.Sqrt,
                    bias=k_eps[:, 4:5]), sSc)
                wait(VEC, sSc)
                op(VEC, lambda ti=ti: v.reciprocal(rstdc[:, ti:ti + 1],
                                                   stat[:, 6:7]), sV)
                wait(VEC, sT)
                op(VEC, lambda sl=sl: v.tensor_scalar(
                    xctm[:], xtm[:, sl, :], stat[:, 2:3], None,
                    ALU.subtract), sV)
                xc_v = sV.n
                wait(TEN, sV, xc_v)
                i4 = ti % 4
                for half in range(2):
                    wait(TEN, sV)
                    wait(VEC, sLD)
                    for dq in range(4):
                        dc = half * 4 + dq
                        op(TEN, lambda dc=dc, dq=dq: t.transpose(
                            p4b[:, dq * 128:(dq + 1) * 128],
                            xctm[:, dc * 128:(dc + 1) * 128], k_i128[:]),
                           sT if dq == 3 else None)
                    tp = sT.n
                    wait(VEC, sT, tp)
                    op(VEC, lambda half=half, i4=i4: v.tensor_copy(
                        xcb[:, half * 4:(half + 1) * 4,
                            i4 * 128:(i4 + 1) * 128],
                        p4b[:, 0:512].rearrange("p (c n) -> p c n", n=128)),
                       sV)
                xcfm_v = sV.n
                wait(SYN, sV, xcfm_v)
                dma(SYN, xcs[:, :, ti * 128: (ti + 1) * 128],
                    xcb[:, :, i4 * 128:(i4 + 1) * 128])

                if nstg < 2:
                    return
                wait(TEN, sGD, wq_g)
                wait(TEN, sV, xcfm_v)
                for dc in range(8):
                    st_, sp_ = dc == 0, dc == 7
                    lhs = xcb[:, dc, i4 * 128:(i4 + 1) * 128]
                    for (pk, Wt) in ((p0, W_q), (p1, W_k), (p2, W_v),
                                     (p3, W_g)):
                        op(TEN, lambda pk=pk, Wt=Wt, lhs=lhs, dc=dc,
                           st_=st_, sp_=sp_: t.matmul(
                            pk[:], lhs, Wt[:, dc, :], start=st_, stop=sp_),
                           None)
                    op(TEN, lambda lhs=lhs, dc=dc, st_=st_, sp_=sp_:
                       t.matmul(p4[:, 0:24], lhs, W_sm[:, dc, :],
                                start=st_, stop=sp_),
                       sT if sp_ else None)
                proj_t = sT.n

                if nstg < 3:
                    return
                wait(SCL, sT, proj_t)
                wait(SCL, sV)
                op(SCL, lambda ti=ti, pb=pb: sc.activation(
                    beta_b[:, pb, :], p4[:, 0:8], AF.Sigmoid,
                    scale=rstdc[:, ti:ti + 1]), sSc)
                wait(VEC, sT, proj_t)
                op(VEC, lambda ti=ti: v.scalar_tensor_tensor(
                    sm_t[:], p4[:, 8:16], rstdc[:, ti:ti + 1], k_dtb[:],
                    ALU.mult, ALU.add), sV)
                wait(SCL, sV)
                wait(SCL, sGP)
                op(SCL, lambda: sc.activation(sm_u[:], sm_t[:], AF.Abs), sSc)
                op(SCL, lambda: sc.activation(sm_u[:], sm_u[:], AF.Exp,
                                              scale=-1.0), sSc)
                op(SCL, lambda: sc.activation(sm_u[:], sm_u[:], AF.Ln,
                                              bias=k_eps[:, 2:3]), sSc)
                wait(VEC, sSc)
                op(VEC, lambda: v.tensor_scalar_max(sm_t[:], sm_t[:], 0.0), sV)
                op(VEC, lambda: v.tensor_add(sm_t[:], sm_t[:], sm_u[:]), sV)
                op(VEC, lambda pb=pb: v.tensor_mul(dec_b[:, pb, :], sm_t[:],
                                                   k_eA[:]), sV)
                op(VEC, lambda pb=pb: v.tensor_copy(dec_bf[:, pb, :],
                                                    dec_b[:, pb, :]), sV)
                dec_v = sV.n
                if i4 == 0:
                    wait(VEC, sLD)
                op(VEC, lambda ti=ti, i4=i4: v.scalar_tensor_tensor(
                    rwwb[:, i4, :], p4[:, 16:24], rstdc[:, ti:ti + 1],
                    k_brr[:], ALU.mult, ALU.add), sV)
                if i4 == 3:
                    m4a = stat[:, 0:4].unsqueeze(2)
                    m4b = stat[:, 4:8].unsqueeze(2)
                    op(VEC, lambda: v.tensor_reduce(
                        m4a, rwwb[:], AX.X, ALU.max, negate=True), sV)
                    op(VEC, lambda: v.tensor_tensor(
                        rt4[:], rwwb[:], bch(stat[:, 0:4], 8), ALU.add), sV)
                    wait(SCL, sV)
                    op(SCL, lambda: sc.activation(
                        rt4[:].rearrange("p a b -> p (a b)"),
                        rt4[:].rearrange("p a b -> p (a b)"), AF.Exp), sSc)
                    wait(VEC, sSc)
                    op(VEC, lambda: v.tensor_reduce(
                        m4b, rt4[:], AX.X, ALU.add), sV)
                    op(VEC, lambda: v.reciprocal(stat[:, 4:8],
                                                 stat[:, 4:8]), sV)
                    op(VEC, lambda: v.tensor_tensor(
                        rt4[:], rt4[:], bch(stat[:, 4:8], 8), ALU.mult), sV)
                    op(VEC, lambda: v.tensor_reduce(
                        m4a, rt4[:], AX.X, ALU.max), sV)
                    op(VEC, lambda: v.tensor_tensor(
                        rwwb[:], rt4[:], bch(stat[:, 0:4], 8),
                        ALU.is_equal), sV)
                    op(VEC, lambda: v.tensor_scalar(
                        rt4b[:].rearrange("p a b -> p (a b)"),
                        rwwb[:].rearrange("p a b -> p (a b)"),
                        -1.0, 1.0, ALU.mult, ALU.add), sV)
                    op(VEC, lambda: v.tensor_mul(
                        rt4b[:].rearrange("p a b -> p (a b)"),
                        rt4[:].rearrange("p a b -> p (a b)"),
                        rt4b[:].rearrange("p a b -> p (a b)")), sV)
                    op(VEC, lambda: v.tensor_reduce(
                        m4b, rt4b[:], AX.X, ALU.max), sV)
                    op(VEC, lambda: v.tensor_tensor(
                        rt4b[:], rt4b[:], bch(stat[:, 4:8], 8),
                        ALU.is_equal), sV)
                    op(VEC, lambda: v.tensor_add(
                        rwwb[:].rearrange("p a b -> p (a b)"),
                        rwwb[:].rearrange("p a b -> p (a b)"),
                        rt4b[:].rearrange("p a b -> p (a b)")), sV)
                    op(VEC, lambda: v.tensor_mul(
                        rwwb[:].rearrange("p a b -> p (a b)"),
                        rt4[:].rearrange("p a b -> p (a b)"),
                        rwwb[:].rearrange("p a b -> p (a b)")), sV)
                    rww_v = sV.n
                    wait(SYN, sV, rww_v)
                    dma(SYN, rww_d[ti - 3:ti + 1].rearrange(
                        "t p e -> p t e"),
                        rwwb[:])

                if nstg < 4:
                    return
                wait(TEN, sV, dec_v)
                op(TEN, lambda pb=pb: t.matmul(
                    p4[0:64, 24:32], k_triu[0:64, :], dec_bf[0:64, pb, :],
                    start=True, stop=False, tile_position=(0, 0)), None)
                op(TEN, lambda pb=pb: t.matmul(
                    p4[0:64, 32:40], k_ustr[0:64, :], dec_bf[0:64, pb, :],
                    start=False, stop=True, tile_position=(0, 0)), None)
                op(TEN, lambda pb=pb: t.matmul(
                    p4[64:128, 24:32], k_triu[64:128, :], dec_bf[64:128, pb, :],
                    start=True, stop=False, tile_position=(64, 64)), None)
                op(TEN, lambda pb=pb: t.matmul(
                    p4[64:128, 32:40], k_ustr[64:128, :], dec_bf[64:128, pb, :],
                    start=False, stop=True, tile_position=(64, 64)), None)
                op(TEN, lambda pb=pb: t.matmul(
                    p4[:, 40:48], k_sum0[:], dec_bf[:, pb, :],
                    start=True, stop=False), None)
                op(TEN, lambda pb=pb: t.matmul(
                    p4[:, 48:56], k_sum1[:], dec_bf[:, pb, :],
                    start=False, stop=True), sT)
                g_t = sT.n
                wait(SCL, sT, g_t)
                wait(SCL, sV)
                wait(SCL, sGP)
                op(SCL, lambda: sc.activation(eg_b[:], p4[:, 24:32], AF.Exp,
                                              scale=-1.0), sSc)
                op(SCL, lambda: sc.activation(eglg_b[:], p4[:, 32:40],
                                              AF.Exp, scale=-1.0), sSc)
                op(SCL, lambda pb=pb: sc.activation(
                    eglr[:, pb].rearrange("p a b -> p (a b)"),
                    p4[:, 40:56], AF.Exp, scale=-1.0), sSc)
                eg_s = sSc.n
                fst[ti]["eg_s"] = eg_s

                if nstg < 5:
                    return
                wait(SCL, sT, proj_t)
                wait(SCL, sV)
                op(SCL, lambda: sc.activation(xsq[:, 0:512], p0[:],
                                              AF.Square), sSc)
                op(SCL, lambda: sc.activation(xctm[:, 0:512], p1[:],
                                              AF.Square), sSc)
                wait(VEC, sSc)
                op(VEC, lambda: v.tensor_reduce(
                    scl_b[:, 0:8].unsqueeze(2),
                    xsq[:, 0:512].rearrange("p (h k) -> p h k", k=64),
                    AX.X, ALU.add), sV)
                op(VEC, lambda: v.tensor_reduce(
                    scl_b[:, 8:16].unsqueeze(2),
                    xctm[:, 0:512].rearrange("p (h k) -> p h k", k=64),
                    AX.X, ALU.add), sV)
                wait(SCL, sV)
                op(SCL, lambda: sc.activation(stat[:, 8:16], scl_b[:, 0:8],
                                              AF.Sqrt, bias=k_eps[:, 0:1],
                                              scale=64.0), sSc)
                op(SCL, lambda: sc.activation(stat[:, 0:8], scl_b[:, 8:16],
                                              AF.Sqrt, bias=k_eps[:, 1:2]), sSc)
                wait(VEC, sSc)
                op(VEC, lambda: v.reciprocal(scl_b[:, 0:8], stat[:, 8:16]), sV)
                op(VEC, lambda: v.reciprocal(scl_b[:, 8:16], stat[:, 0:8]), sV)
                op(VEC, lambda pb=pb: v.tensor_tensor(
                    qn_b[:, pb].rearrange("p (h k) -> p h k", k=64),
                    p0[:].rearrange("p (h k) -> p h k", k=64),
                    bch(scl_b[:, 0:8], 64),
                    ALU.mult), sV)
                op(VEC, lambda pb=pb: v.tensor_tensor(
                    kn_b[:, pb].rearrange("p (h k) -> p h k", k=64),
                    p1[:].rearrange("p (h k) -> p h k", k=64),
                    bch(scl_b[:, 8:16], 64),
                    ALU.mult), sV)

                wait(VEC, sSc, eg_s)
                op(VEC, lambda ti=ti, pb=pb: v.tensor_scalar(
                    sm_t[:], beta_b[:, pb, :], rstdc[:, ti:ti + 1], None,
                    ALU.mult), sV)
                op(VEC, lambda pb=pb: v.tensor_tensor(
                    vb_b[:, pb].rearrange("p (h k) -> p h k", k=64),
                    p2[:].rearrange("p (h k) -> p h k", k=64),
                    bch(sm_t[:], 64), ALU.mult), sV)
                op(VEC, lambda ti=ti, pb=pb: v.tensor_scalar(
                    gt_b[:, pb], p3[:], rstdc[:, ti:ti + 1], None,
                    ALU.mult), sV)
                op(VEC, lambda pb=pb: v.tensor_tensor(
                    qe_b[:, pb].rearrange("p (h k) -> p h k", k=64),
                    qn_b[:, pb].rearrange("p (h k) -> p h k", k=64),
                    bch(eg_b[:], 64), ALU.mult), sV)
                op(VEC, lambda pb=pb: v.tensor_tensor(
                    ke2[:, pb, :, 0:64],
                    kn_b[:, pb].rearrange("p (h k) -> p h k", k=64),
                    bch(eglg_b[:], 64), ALU.mult), sV)
                wait(GPS, sV)
                wait(GPS, sSc)
                op(GPS, lambda pb=pb: gp.tensor_tensor(
                    ke2[:, pb, :, 64:128],
                    kn_b[:, pb].rearrange("p (h k) -> p h k", k=64),
                    bch(eglg_b[:], 64), ALU.mult), sGP)
                wait(VEC, sGP)
                op(VEC, lambda pb=pb: v.tensor_mul(sm_u[:], beta_b[:, pb, :],
                                                   eg_b[:]), sV)
                wait(GPS, sV)
                op(GPS, lambda pb=pb: gp.tensor_tensor(
                    kbg_b[:, pb].rearrange("p (h k) -> p h k", k=64),
                    kn_b[:, pb].rearrange("p (h k) -> p h k", k=64),
                    bch(sm_u[:], 64), ALU.mult), sGP)
                fst[ti]["scal_v"] = sV.n
                fst[ti]["scal_g"] = sGP.n

            def back(ti):
                pb = ti % 2
                i4 = ti % 4
                if nstg < 6 or "scal_v" not in fst[ti]:
                    return
                qn = qn_b[:, pb]
                kn = kn_b[:, pb]
                qe = qe_b[:, pb]
                vb = vb_b[:, pb]
                kbg = kbg_b[:, pb]
                wait(TEN, sV, fst[ti]["scal_v"])
                wait(TEN, sSc)
                for (srcb, dstb) in ((qn, qnK), (qe, qeK), (kn, knK)):
                    wait(TEN, sV)
                    for hp in range(4):
                        op(TEN, lambda srcb=srcb, hp=hp: t.transpose(
                            p6b[:, hp * 128:(hp + 1) * 128],
                            srcb[:, hp * 128:(hp + 1) * 128], k_i128[:]),
                           sT if hp == 3 else None)
                    tp = sT.n
                    wait(VEC, sT, tp)
                    op(VEC, lambda dstb=dstb: v.tensor_copy(
                        dstb[:].rearrange("p a b -> p (a b)"),
                        p6b[:, 0:512]), sV)
                km_v = sV.n
                for (srcK, dstK) in ((qnK, qnKd), (qeK, qeKd), (knK, knKd)):
                    wait(TEN, sV)
                    wait(TEN, sSc)
                    op(TEN, lambda srcK=srcK: t.matmul(
                        p6[:, 0:512], k_swp[:],
                        srcK[:].rearrange("p a b -> p (a b)"),
                        start=True, stop=True), sT)
                    dp = sT.n
                    wait(SCL, sT, dp)
                    op(SCL, lambda dstK=dstK: sc.copy(
                        dstK[:].rearrange("p a b -> p (a b)"),
                        p6[:, 0:512]), sSc)
                kd_s = sSc.n

                if nstg < 7:
                    return
                op(VEC, lambda pb=pb: v.tensor_tensor(
                    rhsU[:].rearrange("p (h m) -> p h m", m=64),
                    bck(k_ustr[:], 8),
                    bch(dec_b[:, pb, :], 64), ALU.mult), sV)
                wait(GPS, sV)
                op(GPS, lambda pb=pb: gp.tensor_tensor(
                    rhsL[:].rearrange("p (h m) -> p h m", m=64),
                    bck(k_triu[:], 8),
                    bch(dec_b[:, pb, :], 64), ALU.mult), sGP)
                lb_v = sV.n
                lb_g = sGP.n
                wait(TEN, sV, lb_v)
                wait(TEN, sGP, lb_g)
                for sp in range(2):
                    r0 = sp * 64
                    op(TEN, lambda r0=r0: t.matmul(
                        p5[r0:r0 + 64, :], k_triu[r0:r0 + 64, :],
                        rhsU[r0:r0 + 64, :], start=True,
                        stop=True, tile_position=(r0, r0)), None)
                    op(TEN, lambda r0=r0: t.matmul(
                        p7[r0:r0 + 64, :], k_ustr[r0:r0 + 64, :],
                        rhsL[r0:r0 + 64, :], start=True,
                        stop=True, tile_position=(r0, r0)),
                       sT if r0 == 64 else None)
                d_t = sT.n
                wait(SCL, sT, d_t)
                wait(SCL, sV)
                op(SCL, lambda: sc.activation(Lst[:], p5[:], AF.Exp,
                                              scale=-1.0), sSc)
                op(SCL, lambda: sc.activation(LTm[:], p7[:], AF.Exp,
                                              scale=-1.0), sSc)
                exp_s = sSc.n
                wait(VEC, sSc, exp_s)
                op(VEC, lambda: v.tensor_tensor(
                    Lst[:].rearrange("p (h m) -> p h m", m=64),
                    Lst[:].rearrange("p (h m) -> p h m", m=64),
                    bck(k_strict[:], 8), ALU.mult), sV)
                wait(GPS, sSc, exp_s)
                op(GPS, lambda: gp.tensor_tensor(
                    LTm[:].rearrange("p (h m) -> p h m", m=64),
                    LTm[:].rearrange("p (h m) -> p h m", m=64),
                    bck(k_trilT[:], 8), ALU.mult), sGP)
                lm_v = sV.n
                lm_g = sGP.n

                if nstg < 8:
                    return
                wait(TEN, sV, max(km_v, lm_v))
                wait(TEN, sGP, lm_g)
                wait(TEN, sSc, kd_s)
                for h in range(8):
                    hp = h // 2
                    for sp in range(2):
                        r0 = sp * 64
                        knS = knK if (h % 2) == sp else knKd
                        qnS = qnK if (h % 2) == sp else qnKd
                        op(TEN, lambda knS=knS, hp=hp, sp=sp, r0=r0, h=h:
                           t.matmul(
                            p5[r0:r0 + 64, h * 64:(h + 1) * 64],
                            knS[r0:r0 + 64, hp, sp * 64:(sp + 1) * 64],
                            knS[r0:r0 + 64, hp, sp * 64:(sp + 1) * 64],
                            start=(h == 0), stop=(h == 7),
                            tile_position=(r0, r0)), None)
                        op(TEN, lambda knS=knS, qnS=qnS, hp=hp, sp=sp,
                           r0=r0, h=h: t.matmul(
                            p7[r0:r0 + 64, h * 64:(h + 1) * 64],
                            knS[r0:r0 + 64, hp, sp * 64:(sp + 1) * 64],
                            qnS[r0:r0 + 64, hp, sp * 64:(sp + 1) * 64],
                            start=(h == 0), stop=(h == 7),
                            tile_position=(r0, r0)),
                           sT if (h == 7 and sp == 1) else None)
                ga_t = sT.n
                wait(VEC, sT, ga_t)
                op(VEC, lambda pb=pb: v.tensor_tensor(
                    Pk_s[:].rearrange("p (h m) -> p h m", m=64),
                    p5[:].rearrange("p (h m) -> p h m", m=64),
                    bch(beta_b[:, pb, :], 64), ALU.mult), sV)
                op(VEC, lambda: v.tensor_mul(Pk_s[:], Pk_s[:], Lst[:]), sV)
                wait(VEC, sGP, lm_g)
                op(VEC, lambda: v.tensor_mul(attT[:], p7[:], LTm[:]), sV)
                pb_v = sV.n

                if nstg < 9:
                    return
                wait(TEN, sV, pb_v)
                for h in range(8):
                    for sp in range(2):
                        r0 = sp * 64
                        op(TEN, lambda h=h, r0=r0: t.transpose(
                            p6b[r0:r0 + 64, h * 64:(h + 1) * 64],
                            Pk_s[r0:r0 + 64, h * 64:(h + 1) * 64],
                            k_i128[r0:r0 + 64, r0:r0 + 64],
                            tile_position=(r0, r0)),
                           sT if (h == 7 and r0 == 64) else None)
                it_t = sT.n
                wait(VEC, sT, it_t)
                op(VEC, lambda: v.tensor_copy(PkT_s[:], p6b[:, 0:512]), sV)
                op(VEC, lambda: v.scalar_tensor_tensor(
                    Av_s[:].rearrange("p (h m) -> p h m", m=64),
                    Pk_s[:].rearrange("p (h m) -> p h m", m=64),
                    1.0, bck(k_i64s[:], 8), ALU.mult, ALU.add), sV)
                ser_v = sV.n
                ser_c = sSc.n
                for it in range(6):
                    wait(TEN, sV, ser_v)
                    wait(TEN, sSc, ser_c)
                    for h in range(8):
                        hs = slice(h * 64, (h + 1) * 64)
                        for sp in range(2):
                            r0 = sp * 64
                            first = (h == 0)
                            last = (h == 7)
                            if it < 5:
                                op(TEN, lambda hs=hs, r0=r0, first=first,
                                   last=last: t.matmul(
                                    p5[r0:r0 + 64, hs], PkT_s[r0:r0 + 64, hs],
                                    Pk_s[r0:r0 + 64, hs], start=first,
                                    stop=last, tile_position=(r0, r0)), None)
                                op(TEN, lambda hs=hs, r0=r0, first=first,
                                   last=last: t.matmul(
                                    p7[r0:r0 + 64, hs], Pk_s[r0:r0 + 64, hs],
                                    PkT_s[r0:r0 + 64, hs], start=first,
                                    stop=last,
                                    tile_position=(r0, r0)),
                                   sT if (it == 0 and last and sp == 1)
                                   else None)
                            if it >= 1:
                                op(TEN, lambda hs=hs, r0=r0, first=first,
                                   last=last: t.matmul(
                                    p6[r0:r0 + 64, hs], PkT_s[r0:r0 + 64, hs],
                                    Av_s[r0:r0 + 64, hs], start=first,
                                    stop=last, tile_position=(r0, r0)),
                                   sT if (last and sp == 1) else None)
                    si_t = sT.n
                    wait(VEC, sT, si_t)
                    wait(SCL, sT, si_t)
                    if it < 5:
                        op(VEC, lambda: v.tensor_copy(Pk_s[:], p5[:]), sV)
                        op(SCL, lambda: sc.copy(PkT_s[:], p7[:]), sSc)
                    if it >= 1:
                        op(VEC, lambda: v.scalar_tensor_tensor(
                            Av_s[:], p6[:, 0:512], 1.0, Av_s[:],
                            ALU.mult, ALU.add), sV)
                    ser_v = sV.n
                    ser_c = sSc.n
                    wait(TEN, sSc, ser_c)
                wait(TEN, sV, ser_v)
                wait(TEN, sSc, ser_c)
                for h in range(8):
                    for sp in range(2):
                        r0 = sp * 64
                        op(TEN, lambda h=h, r0=r0: t.transpose(
                            p6b[r0:r0 + 64, h * 64:(h + 1) * 64],
                            Av_s[r0:r0 + 64, h * 64:(h + 1) * 64],
                            k_i128[r0:r0 + 64, r0:r0 + 64],
                            tile_position=(r0, r0)),
                           sT if (h == 7 and r0 == 64) else None)
                at_t = sT.n
                wait(VEC, sT, at_t)
                op(VEC, lambda: v.tensor_copy(AvT_s[:], p6b[:, 0:512]), sV)
                avt_v = sV.n

                if nstg < 10:
                    return
                wait(TEN, sV, avt_v)
                wait(TEN, sGP, fst[ti]["scal_g"])
                for h in range(8):
                    hs = slice(h * 64, (h + 1) * 64)
                    for sp in range(2):
                        r0 = sp * 64
                        op(TEN, lambda r0=r0, hs=hs, h=h, vb=vb: t.matmul(
                            p5[r0:r0 + 64, hs], AvT_s[r0:r0 + 64, hs],
                            vb[r0:r0 + 64, hs],
                            start=(h == 0), stop=(h == 7),
                            tile_position=(r0, r0)), None)
                        op(TEN, lambda r0=r0, hs=hs, h=h, kbg=kbg: t.matmul(
                            p7[r0:r0 + 64, hs],
                            kbg[r0:r0 + 64, hs], AvT_s[r0:r0 + 64, hs],
                            start=(h == 0), stop=(h == 7),
                            tile_position=(r0, r0)),
                           sT if (h == 7 and sp == 1) else None)
                uw_t = sT.n
                wait(VEC, sT, uw_t)
                op(VEC, lambda: v.tensor_copy(u_s[:], p5[:]), sV)
                op(VEC, lambda: v.tensor_copy(w_s[:], p7[:]), sV)
                uw_v = sV.n

                if nstg < 11:
                    return
                wait(SCL, sV)
                op(SCL, lambda pb=pb: sc.activation(xsq2[:], gt_b[:, pb],
                                                    AF.Silu), sSc)
                wait(TEN, sGP, fst[ti]["scal_g"])
                wait(VEC, sSc, fst[ti]["eg_s"])
                for sp in range(2):
                    r0 = sp * 64
                    vn = vn0 if sp == 0 else vn1
                    wait(TEN, sV)
                    for h in range(8):
                        hs = slice(h * 64, (h + 1) * 64)
                        op(TEN, lambda hs=hs, r0=r0, h=h: t.matmul(
                            p6[r0:r0 + 64, hs], w_s[r0:r0 + 64, hs],
                            S_bf[r0:r0 + 64, hs],
                            start=(h == 0), stop=(h == 7),
                            tile_position=(r0, r0)),
                           sT if h == 7 else None)
                    vn_t = sT.n
                    wait(VEC, sT, vn_t)
                    op(VEC, lambda r0=r0, vn=vn: v.scalar_tensor_tensor(
                        vn[r0:r0 + 64, :], p6[r0:r0 + 64, 0:512], -1.0,
                        u_s[r0:r0 + 64, :], ALU.mult, ALU.add), sV)
                    vn_v = sV.n
                    wait(TEN, sV, vn_v)
                    for h in range(8):
                        hp = h // 2
                        hs = slice(h * 64, (h + 1) * 64)
                        qeS = qeK if (h % 2) == sp else qeKd
                        op(TEN, lambda qeS=qeS, hs=hs, hp=hp, sp=sp, r0=r0,
                           h=h: t.matmul(
                            p7[r0:r0 + 64, hs],
                            qeS[r0:r0 + 64, hp, sp * 64:(sp + 1) * 64],
                            S_bf[r0:r0 + 64, hs],
                            start=(h == 0), stop=False,
                            tile_position=(r0, r0)), None)
                        op(TEN, lambda hs=hs, r0=r0, h=h, vn=vn: t.matmul(
                            p7[r0:r0 + 64, hs], attT[r0:r0 + 64, hs],
                            vn[r0:r0 + 64, hs], start=False,
                            stop=(h == 7), tile_position=(r0, r0)), None)
                        op(TEN, lambda hs=hs, h=h, vn=vn, pb=pb: t.matmul(
                            p5[:, hs], ke2[:, pb, h, :], vn[:, hs],
                            start=(h == 0), stop=(h == 7)),
                           sT if h == 7 else None)
                    su_t = sT.n
                    wait(SCL, sT, su_t)
                    op(SCL, lambda r0=r0: sc.copy(
                        o_b[r0:r0 + 64, :], p7[r0:r0 + 64, :]), sSc)
                    wait(VEC, sT, su_t)
                    op(VEC, lambda sp=sp, pb=pb: v.tensor_tensor(
                        Stmp[:].rearrange("p (a b) -> p a b", b=64),
                        S_st[:].rearrange("p (a b) -> p a b", b=64),
                        bch(eglr[:, pb, sp, :], 64),
                        ALU.mult), sV)
                    op(VEC, lambda: v.tensor_add(S_st[:], Stmp[:],
                                                 p5[:, 0:512]), sV)
                    wait(SCL, sV)
                    op(SCL, lambda: sc.copy(S_bf[:], S_st[:]), sSc)
                    wait(TEN, sSc)

                if nstg < 12:
                    return
                wait(SCL, sV)
                op(SCL, lambda: sc.activation(xsq[:, 0:512], o_b[:],
                                              AF.Square), sSc)
                wait(VEC, sSc)
                op(VEC, lambda: v.tensor_reduce(
                    ms_b[:].unsqueeze(2),
                    xsq[:, 0:512].rearrange("p (h k) -> p h k", k=64),
                    AX.X, ALU.add), sV)
                wait(SCL, sV)
                op(SCL, lambda: sc.activation(ms_b[:], ms_b[:], AF.Sqrt,
                                              bias=k_eps[:, 4:5],
                                              scale=1.0 / 64), sSc)
                wait(VEC, sSc)
                op(VEC, lambda: v.reciprocal(ms_b[:], ms_b[:]), sV)
                op(VEC, lambda: v.tensor_mul(og_b[:], o_b[:],
                                             xsq2[:]), sV)
                op(VEC, lambda: v.tensor_tensor(
                    og_b[:].rearrange("p (h k) -> p h k", k=64),
                    og_b[:].rearrange("p (h k) -> p h k", k=64),
                    bch(ms_b[:], 64), ALU.mult), sV)
                wait(GPS, sV)
                op(GPS, lambda: gp.tensor_tensor(
                    og_b[:].rearrange("p (h k) -> p h k", k=64),
                    og_b[:].rearrange("p (h k) -> p h k", k=64),
                    bck(k_nw[:], 8), ALU.mult), sGP)
                og_v = sV.n
                og_g = sGP.n
                wait(TEN, sV, og_v)
                wait(TEN, sGP, og_g)
                for hp in range(4):
                    op(TEN, lambda hp=hp: t.transpose(
                        p6b[:, hp * 128:(hp + 1) * 128],
                        og_b[:, hp * 128:(hp + 1) * 128], k_i128[:]),
                       sT if hp == 3 else None)
                ogt_t = sT.n
                wait(SCL, sT, ogt_t)
                wait(SCL, sLD)
                op(SCL, lambda i4=i4: sc.copy(
                    ogf_b[:, :, i4 * 128:(i4 + 1) * 128],
                    p6b[:, 0:512].rearrange("p (a b) -> p a b", b=128)), sSc)
                if i4 == 3:
                    t0 = (ti // 4) * 512
                    ogf_v2 = sSc.n
                    wait(SYN, sSc, ogf_v2)
                    for hp in range(4):
                        dma(SYN, ogall[hp * 128:(hp + 1) * 128, t0:t0 + 512],
                            ogf_b[:, hp, :])

            NTI = min(NT, 4 * nblk) if nblk < NT else NT
            wait(GPS, sV)
            wait(GPS, sSc)
            dma(GPS, xtm[:, 0, :], xTM[0:128, :])
            for ti in range(NTI):
                fst[ti] = {"xl": sGD.n}
                front(ti)
                if ti >= 1:
                    back(ti - 1)
                    del fst[ti - 1]
            if NTI >= 1:
                back(NTI - 1)
            if phases < 2:
                op(VEC, lambda: v.memset(o_t1[:], 0.0), sV)
                wait(SYN, sV)
                for ob in range(NOB):
                    for tt in range(4):
                        dma(SYN, outY[ob * 512 + tt * 128: ob * 512 + (tt + 1) * 128, :],
                            o_t1[:], sST)
                wait(SYN, sST)
                return
            wait(GPS, sLD)
            rank = None
            if me == GPS:
                rank = gp.cc_rank(PAIR_GROUPS)

            def ccin_dma():
                src = ogall[:, 0:OWNT]
                src = dataclasses.replace(src, offset=(1 - rank) * OWNT)
                return gp.dma_start(out=cc_in.rearrange("(a b) -> a b", a=512),
                                    in_=src)
            dmaop(GPS, ccin_dma)
            wait(GPS, sGD)
            op(GPS, lambda: gp.collective_compute(
                "AllGather", ALU.bypass, replica_groups=PAIR_GROUPS,
                ins=[cc_in], outs=[cc_out]), sCC)
            wait(GPS, sCC)
            wait(SYN, sCC)
            wait(TEN, sCC)

            rank_s = None
            if me == SYN:
                rank_s = sy.cc_rank(PAIR_GROUPS)

            def cp_ogown():
                src = dataclasses.replace(ogall[:, 0:OWNT], offset=rank_s * OWNT)
                return sy.dma_start(out=og_own, in_=src)
            dmaop(SYN, cp_ogown)

            def cp_ogpar():
                src = dataclasses.replace(cc_out[0:512 * OWNT],
                                          offset=(1 - rank_s) * (512 * OWNT))
                return sy.dma_start(out=og_par.rearrange("a b -> (a b)"), in_=src)
            dmaop(SYN, cp_ogpar)

            def cp_xcs():
                src = dataclasses.replace(xcs[:, :, 0:OWNT], offset=rank_s * OWNT)
                return sy.dma_start(out=xcs_own, in_=src)
            dmaop(SYN, cp_xcs)

            def cp_rww():
                src = dataclasses.replace(
                    rww_d[0:NT // 2], offset=rank_s * (NT // 2) * (128 * 8))
                return sy.dma_start(out=rww_own, in_=src)
            dmaop(SYN, cp_rww)
            wait(SYN, sLD)

            if phases < 3:
                op(VEC, lambda: v.memset(o_t1[:], 0.0), sV)
                wait(SYN, sV)
                for ob in range(NOB):
                    for tt in range(4):
                        dma(SYN, outY[ob * 512 + tt * 128: ob * 512 + (tt + 1) * 128, :],
                            o_t1[:], sST)
                wait(SYN, sST)
                return
            for dc in range(8):
                dma(GPS, W_e[:, 0, dc, :], wo[dc * 128:(dc + 1) * 128, :])
            wo_g = sGD.n
            wait(TEN, sGD, wo_g)
            for ob in range(NOB):
                wait(SYN, sT)
                for hp in range(4):
                    dma(SYN, xcb[:, hp, :],
                        og_own[hp * 128:(hp + 1) * 128, ob * 512:(ob + 1) * 512])
                    dma(SYN, xcb[:, 4 + hp, :],
                        og_par[hp * 128:(hp + 1) * 128, ob * 512:(ob + 1) * 512])
                ogl = sLD.n
                wait(TEN, sLD, ogl)
                for tt in range(4):
                    ot = ob * 4 + tt
                    wait(TEN, sV)
                    for dc in range(8):
                        for jh in range(2):
                            op(TEN, lambda tt=tt, jh=jh, dc=dc: t.matmul(
                                (p0 if jh == 0 else p1)[:],
                                xcb[:, dc, tt * 128:(tt + 1) * 128],
                                W_e[:, 0, dc, jh * 512:(jh + 1) * 512],
                                start=(dc == 0), stop=(dc == 7)),
                               sT if (dc == 7 and jh == 1) else None)
                    wo_t = sT.n
                    wait(VEC, sT, wo_t)
                    wait(VEC, sST)
                    op(VEC, lambda: v.tensor_copy(og_b[:], p0[:]), sV)
                    op(VEC, lambda: v.tensor_copy(o_b[:], p1[:]), sV)
                    at_v = sV.n
                    wait(SYN, sV, at_v)
                    dma(SYN, attn_d[ot, :, 0:512], og_b[:], sST)
                    dma(SYN, attn_d[ot, :, 512:1024], o_b[:], sST)

            if phases < 4:
                op(VEC, lambda: v.memset(o_t1[:], 0.0), sV)
                wait(SYN, sV)
                for ob in range(NOB):
                    for tt in range(4):
                        dma(SYN, outY[ob * 512 + tt * 128: ob * 512 + (tt + 1) * 128, :],
                            o_t1[:], sST)
                wait(SYN, sST)
                return
            prevmm = sT.n
            wuser = [prevmm, prevmm]
            for ob in range(NOB):
                wait(SYN, sT, prevmm)
                for dc in range(8):
                    dma(SYN, xcb[:, dc, :], xcs_own[:, dc, ob * 512:(ob + 1) * 512])
                for tt in range(4):
                    dma(SYN, rwwb[:, tt, :], rww_own[ob * 4 + tt])
                xcl = sLD.n
                wait(VEC, sLD, xcl)
                op(VEC, lambda: v.tensor_copy(
                    rwwb_bf[:].rearrange("p a b -> p (a b)"),
                    rwwb[:].rearrange("p a b -> p (a b)")), sV)
                ev_half = [sV.n, sV.n]
                mmdone = []
                for e in range(8):
                    wait(GPS, sT, mmdone[e - 2] if e >= 2 else wuser[e % 2])
                    for dc in range(8):
                        def ld_we(e=e, dc=dc):
                            return gp.dma_start(
                                out=W_e[:, e % 2, dc, :],
                                in_=weT[e, dc * 128:(dc + 1) * 128, :])
                        dmaop(GPS, ld_we)
                    wsm_n = sGD.n
                    wait(TEN, sGD, wsm_n)
                    if e == 0:
                        wait(TEN, sLD, xcl)
                    for th in range(2):
                        wait(TEN, sV, ev_half[th])
                        for tt in (th * 2, th * 2 + 1):
                            for dc in range(8):
                                for jh in range(2):
                                    bank = banks[tt * 2 + jh]
                                    op(TEN, lambda bank=bank, tt=tt, dc=dc,
                                       jh=jh, e=e: t.matmul(
                                        bank[:],
                                        xcb[:, dc, tt * 128:(tt + 1) * 128],
                                        W_e[:, e % 2, dc, jh * 512:(jh + 1) * 512],
                                        start=(dc == 0), stop=(dc == 7)),
                                       sT if (tt == th * 2 + 1 and dc == 7
                                              and jh == 1) else None)
                        mm_half = sT.n
                        wait(VEC, sT, mm_half)
                        for tt in (th * 2, th * 2 + 1):
                            for jh in range(2):
                                bank = banks[tt * 2 + jh]
                                dst = ffa[:, tt, jh * 512:(jh + 1) * 512]
                                if e == 0:
                                    op(VEC, lambda bank=bank, dst=dst, tt=tt,
                                       e=e: v.tensor_scalar(
                                        dst, bank[:], rwwb[:, tt, e:e + 1],
                                        None, ALU.mult), sV)
                                else:
                                    op(VEC, lambda bank=bank, dst=dst, tt=tt,
                                       e=e: v.scalar_tensor_tensor(
                                        dst, bank[:], rwwb[:, tt, e:e + 1],
                                        dst, ALU.mult, ALU.add), sV)
                        ev_half[th] = sV.n
                    mmdone.append(sT.n)
                wuser = [mmdone[6], mmdone[7]]
                for tt in range(4):
                    ot = ob * 4 + tt

                    def ld_at(ot=ot):
                        return sy.dma_start(out=atile[:], in_=attn_d[ot])
                    wait(SYN, sV)
                    dmaop(SYN, ld_at)
                    dma(SYN, xtf[:],
                        xOWN[ob * 512 + tt * 128: ob * 512 + (tt + 1) * 128, :])
                    xfl = sLD.n
                    wait(TEN, sV, ev_half[1])
                    op(TEN, lambda tt=tt: t.transpose(
                        p6b[0:8, 0:128], rwwb_bf[:, tt, :], k_i128[:]), sT)
                    rwt_t = sT.n
                    wait(SCL, sT, rwt_t)
                    op(SCL, lambda: sc.copy(rwT_ob[0:8, :],
                                            p6b[0:8, 0:128]), sSc)
                    rwg = sSc.n
                    wait(TEN, sSc, rwg)
                    wait(TEN, sV, ev_half[tt // 2])
                    for jh in range(2):
                        op(TEN, lambda tt=tt, jh=jh: t.matmul(
                            banks[tt * 2 + jh][:], rwT_ob[0:8, :],
                            B_e[:, jh * 512:(jh + 1) * 512], start=True,
                            stop=True), sT if jh == 1 else None)
                    fb_t = sT.n
                    wait(VEC, sT, fb_t)
                    wait(VEC, sLD, xfl)
                    wait(VEC, sST)
                    for jh in range(2):
                        op(VEC, lambda jh=jh, tt=tt: v.tensor_add(
                            o_t1[:, jh * 512:(jh + 1) * 512],
                            ffa[:, tt, jh * 512:(jh + 1) * 512],
                            banks[tt * 2 + jh][:]), sV)
                    op(VEC, lambda tt=tt: v.tensor_add(
                        o_t1[:].rearrange("p (a b) -> p a b", a=2),
                        o_t1[:].rearrange("p (a b) -> p a b", a=2),
                        atile[:].rearrange("p (a b) -> p a b", a=2)), sV)
                    op(VEC, lambda: v.tensor_add(o_t1[:], o_t1[:], xtf[:]), sV)
                    out_v = sV.n
                    wait(SYN, sV, out_v)
                    dma(SYN, outY[ob * 512 + tt * 128: ob * 512 + (tt + 1) * 128, :],
                        o_t1[:], sST)
                prevmm = sT.n
            wait(SYN, sST)

        @block.sync
        def _(_e):
            program("sync")

        @block.gpsimd
        def _(_e):
            program("gpsimd")

        @block.vector
        def _(_e):
            program("vector")

        @block.scalar
        def _(_e):
            program("scalar")

        @block.tensor
        def _(_e):
            program("tensor")

    return nc


def _consts():
    tri = np.triu(np.ones((C, C), np.float32))
    ust = np.tril(np.ones((C, C), np.float32), -1)
    stc = np.tril(np.ones((C, C), np.float32), -1)
    tlt = np.triu(np.ones((C, C), np.float32))
    bf = ml_dtypes.bfloat16
    return {
        "c_triu": np.tile(tri, (2, 1)).astype(bf),
        "c_ustr": np.tile(ust, (2, 1)).astype(bf),
        "c_strict": np.tile(stc, (2, 1)).astype(bf),
        "c_trilT": np.tile(tlt, (2, 1)).astype(bf),
        "csum0": np.vstack([np.ones((64, 128)), np.zeros((64, 128))]).astype(bf),
        "csum1": np.vstack([np.zeros((64, 128)), np.ones((64, 128))]).astype(bf),
        "cswp": np.roll(np.eye(128), 64, axis=0).astype(bf),
        "c_i128": np.eye(128).astype(bf),
        "c_i64s": np.tile(np.eye(64, dtype=np.float32), (2, 1)),
    }


def kernel_impl(inputs, n_cores, runner):
    x = np.asarray(inputs["L0_attn_x"], np.float32)
    Bb, T, _ = x.shape
    Wq = np.asarray(inputs["L0_attn_Wq"], np.float32)
    Wk = np.asarray(inputs["L0_attn_Wk"], np.float32)
    Wv = np.asarray(inputs["L0_attn_Wv"], np.float32)
    Wg = np.asarray(inputs["L0_attn_Wg"], np.float32)
    Wo = np.asarray(inputs["L0_attn_Wo"], np.float32)
    Wb = np.asarray(inputs["L0_attn_Wb"], np.float32)
    Wa = np.asarray(inputs["L0_attn_Wa"], np.float32)
    dt_bias = np.asarray(inputs["L0_attn_dt_bias"], np.float32)
    A_log = np.asarray(inputs["L0_attn_A_log"], np.float32)
    norm_w = np.asarray(inputs["L0_attn_norm_w"], np.float32)
    We = np.asarray(inputs["L0_ffn_W_experts"], np.float32)
    be = np.asarray(inputs["L0_ffn_b_experts"], np.float32)
    Wr = np.asarray(inputs["L0_ffn_W_router"], np.float32)
    br = np.asarray(inputs["L0_ffn_b_router"], np.float32)

    consts = _consts()
    weT = np.ascontiguousarray(We.transpose(0, 2, 1))
    in_maps = []
    for cidx in range(n_cores):
        b, g = cidx // 2, cidx % 2
        hs = slice(g * 512, (g + 1) * 512)
        h8 = slice(g * 8, (g + 1) * 8)
        im = {
            "xTM": np.ascontiguousarray(x[b]),
            "wq": np.ascontiguousarray(Wq[:, hs]),
            "wk": np.ascontiguousarray(Wk[:, hs]),
            "wv": np.ascontiguousarray(Wv[:, hs]),
            "wg": np.ascontiguousarray(Wg[:, hs]),
            "wsm": np.ascontiguousarray(np.concatenate(
                [Wb[:, h8], Wa[:, h8], Wr.T], axis=1)),
            "wo": np.ascontiguousarray(np.concatenate(
                [Wo[g * 512:(g + 1) * 512], Wo[(1 - g) * 512:(2 - g) * 512]],
                axis=0)),
            "weT": weT,
            "bexp": be,
            "dtb": np.ascontiguousarray(np.broadcast_to(dt_bias[h8], (128, 8))),
            "alog": np.ascontiguousarray(np.broadcast_to(A_log[h8], (128, 8))),
            "brr": np.ascontiguousarray(np.broadcast_to(br, (128, 8))),
            "nwr": np.ascontiguousarray(np.broadcast_to(norm_w, (128, 64))),
        }
        im["epsc"] = np.ascontiguousarray(np.broadcast_to(
            np.array([64e-6, 1e-6, 1.0, 0.0, 1e-5, 0.0], np.float32),
            (128, 6)))
        im["xOWN"] = np.ascontiguousarray(x[b, g * (T // 2):(g + 1) * (T // 2)])
        im.update(consts)
        in_maps.append(im)

    nc = build_nc(T=T, n_cores=n_cores)
    res = runner(nc, in_maps)
    out = np.zeros((Bb, T, D), np.float32)
    half = T // 2
    for cidx in range(n_cores):
        b, g = cidx // 2, cidx % 2
        out[b, g * half:(g + 1) * half] = res[cidx]["outY"]
    return out


def kernel(**inputs):
    def runner(nc, in_maps):
        r = run_bass_kernel_spmd(nc, in_maps, core_ids=list(range(len(in_maps))))
        return r.results
    return kernel_impl(inputs, 8, runner)


# version 16 (final)
# speedup vs baseline: 1.0207x; 1.0066x over previous
import dataclasses
import numpy as np
import ml_dtypes

import concourse.bass as bass
import concourse.mybir as mybir
from concourse.bass_utils import run_bass_kernel_spmd

F32 = mybir.dt.float32
BF16 = mybir.dt.bfloat16
AF = mybir.ActivationFunctionType
ALU = mybir.AluOpType
AX = mybir.AxisListType

K, C, D = 64, 64, 1024

TEN, VEC, SCL, GPS, SYN = "tensor", "vector", "scalar", "gpsimd", "sync"


class Sem:
    def __init__(self, h):
        self.h = h
        self.n = 0


DMA_DEPTH = 8


def build_nc(T=8192, n_cores=8, phases=4, nblk=10**9, nstg=99):
    NT = T // 128
    NB = T // 512
    OWNT = T // 2
    NOB = OWNT // 512
    PAIR_GROUPS = [[i, i + 1] for i in range(0, n_cores, 2)]
    nc = bass.Bass("TRN2", target_bir_lowering=False, debug=False, num_devices=n_cores)

    xTM = nc.dram_tensor("xTM", [T, D], F32, kind="ExternalInput").ap()
    wq = nc.dram_tensor("wq", [D, 512], F32, kind="ExternalInput").ap()
    wk = nc.dram_tensor("wk", [D, 512], F32, kind="ExternalInput").ap()
    wv = nc.dram_tensor("wv", [D, 512], F32, kind="ExternalInput").ap()
    wg = nc.dram_tensor("wg", [D, 512], F32, kind="ExternalInput").ap()
    wsm = nc.dram_tensor("wsm", [D, 24], F32, kind="ExternalInput").ap()
    wo = nc.dram_tensor("wo", [D, D], F32, kind="ExternalInput").ap()
    weT = nc.dram_tensor("weT", [8, D, D], F32, kind="ExternalInput").ap()
    bexp = nc.dram_tensor("bexp", [8, D], F32, kind="ExternalInput").ap()
    dtb = nc.dram_tensor("dtb", [128, 8], F32, kind="ExternalInput").ap()
    alog = nc.dram_tensor("alog", [128, 8], F32, kind="ExternalInput").ap()
    brr = nc.dram_tensor("brr", [128, 8], F32, kind="ExternalInput").ap()
    nwr = nc.dram_tensor("nwr", [128, 64], F32, kind="ExternalInput").ap()
    c_triu = nc.dram_tensor("c_triu", [128, 64], BF16, kind="ExternalInput").ap()
    c_ustr = nc.dram_tensor("c_ustr", [128, 64], BF16, kind="ExternalInput").ap()
    c_strict = nc.dram_tensor("c_strict", [128, 64], BF16, kind="ExternalInput").ap()
    c_trilT = nc.dram_tensor("c_trilT", [128, 64], BF16, kind="ExternalInput").ap()
    csum0 = nc.dram_tensor("csum0", [128, 128], BF16, kind="ExternalInput").ap()
    csum1 = nc.dram_tensor("csum1", [128, 128], BF16, kind="ExternalInput").ap()
    cswp = nc.dram_tensor("cswp", [128, 128], BF16, kind="ExternalInput").ap()
    c_i128 = nc.dram_tensor("c_i128", [128, 128], BF16, kind="ExternalInput").ap()
    c_i64s = nc.dram_tensor("c_i64s", [128, 64], F32, kind="ExternalInput").ap()
    epsc = nc.dram_tensor("epsc", [128, 6], F32, kind="ExternalInput").ap()

    ogall = nc.dram_tensor("ogall", [512, T], BF16, kind="Internal").ap()
    xcs = nc.dram_tensor("xcs", [128, 8, T], BF16, kind="Internal").ap()
    rww_d = nc.dram_tensor("rww_d", [NT, 128, 8], F32, kind="Internal").ap()
    attn_d = nc.dram_tensor("attn_d", [OWNT // 128, 128, D], BF16, kind="Internal").ap()
    xOWN = nc.dram_tensor("xOWN", [OWNT, D], F32, kind="ExternalInput").ap()
    og_own = nc.dram_tensor("og_own", [512, OWNT], BF16, kind="Internal").ap()
    og_par = nc.dram_tensor("og_par", [512, OWNT], BF16, kind="Internal").ap()
    xcs_own = nc.dram_tensor("xcs_own", [128, 8, OWNT], BF16, kind="Internal").ap()
    rww_own = nc.dram_tensor("rww_own", [NT // 2, 128, 8], F32, kind="Internal").ap()
    cc_in = nc.dram_tensor("cc_in", [512 * OWNT], BF16, kind="Internal").ap()
    cc_out = nc.dram_tensor("cc_out", [2 * 512 * OWNT], BF16, kind="Internal").ap()
    outY = nc.dram_tensor("outY", [OWNT, D], F32, kind="ExternalOutput").ap()

    from contextlib import ExitStack
    es = ExitStack()
    SB = lambda name, shape, dt=F32: es.enter_context(nc.sbuf_tensor(name, shape, dt))
    PS = lambda name, shape, dt=F32: es.enter_context(nc.psum_tensor(name, shape, dt))
    SEM = lambda name: Sem(es.enter_context(nc.semaphore(name)))

    with es:
        k_triu = SB("k_triu", [128, 64], BF16)
        k_ustr = SB("k_ustr", [128, 64], BF16)
        k_strict = SB("k_strict", [128, 64], BF16)
        k_trilT = SB("k_trilT", [128, 64], BF16)
        k_sum0 = SB("k_sum0", [128, 128], BF16)
        k_sum1 = SB("k_sum1", [128, 128], BF16)
        k_swp = SB("k_swp", [128, 128], BF16)
        k_i128 = SB("k_i128", [128, 128], BF16)
        k_i64s = SB("k_i64s", [128, 64], F32)
        k_eps = SB("k_eps", [128, 6], F32)
        k_dtb = SB("k_dtb", [128, 8], F32)
        k_alog = SB("k_alog", [128, 8], F32)
        k_eA = SB("k_eA", [128, 8], F32)
        k_brr = SB("k_brr", [128, 8], F32)
        k_nw = SB("k_nw", [128, 64], F32)
        W_q = SB("W_q", [128, 8, 512], BF16)
        W_k = SB("W_k", [128, 8, 512], BF16)
        W_v = SB("W_v", [128, 8, 512], BF16)
        W_g = SB("W_g", [128, 8, 512], BF16)
        W_sm = SB("W_sm", [128, 8, 24], BF16)
        W_e = SB("W_e", [128, 2, 8, 1024], BF16)
        B_e = SB("B_e", [8, 1024], BF16)
        rstdc = SB("rstdc", [128, NT], F32)
        S_st = SB("S_st", [128, 512], F32)
        S_bf = SB("S_bf", [128, 512], BF16)
        Stmp = SB("Stmp", [128, 512], F32)
        eglr = SB("eglr", [128, 2, 2, 8], F32)
        xtm = SB("xtm", [128, 2, 1024], BF16)
        xsq = SB("xsq", [128, 1024], BF16)
        xsq2 = SB("xsq2", [128, 512], BF16)
        stat = SB("stat", [128, 16], F32)
        xcb = SB("xcb", [128, 8, 512], BF16)
        xctm = SB("xctm", [128, 1024], BF16)
        qn_b = SB("qn_b", [128, 2, 512], BF16)
        kn_b = SB("kn_b", [128, 2, 512], BF16)
        vb_b = SB("vb_b", [128, 2, 512], BF16)
        kbg_b = SB("kbg_b", [128, 2, 512], BF16)
        qe_b = SB("qe_b", [128, 2, 512], BF16)
        ke2 = SB("ke2", [128, 2, 8, 128], BF16)
        gt_b = SB("gt_b", [128, 2, 512], BF16)
        qnK = SB("qnK", [128, 4, 128], BF16)
        qeK = SB("qeK", [128, 4, 128], BF16)
        knK = SB("knK", [128, 4, 128], BF16)
        qnKd = SB("qnKd", [128, 4, 128], BF16)
        qeKd = SB("qeKd", [128, 4, 128], BF16)
        knKd = SB("knKd", [128, 4, 128], BF16)
        beta_b = SB("beta_b", [128, 2, 8], F32)
        dec_b = SB("dec_b", [128, 2, 8], F32)
        dec_bf = SB("dec_bf", [128, 2, 8], BF16)
        eg_b = SB("eg_b", [128, 8], F32)
        eglg_b = SB("eglg_b", [128, 8], F32)
        scl_b = SB("scl_b", [128, 16], F32)
        rhsU = SB("rhsU", [128, 512], BF16)
        rhsL = SB("rhsL", [128, 512], BF16)
        Lst = SB("Lst", [128, 512], BF16)
        LTm = SB("LTm", [128, 512], BF16)
        attT = SB("attT", [128, 512], BF16)
        Pk_s = SB("Pk_s", [128, 512], BF16)
        PkT_s = SB("PkT_s", [128, 512], BF16)
        Av_s = SB("Av_s", [128, 512], BF16)
        AvT_s = SB("AvT_s", [128, 512], BF16)
        u_s = SB("u_s", [128, 512], BF16)
        w_s = SB("w_s", [128, 512], BF16)
        vn0 = SB("vn0", [128, 512], BF16)
        vn1 = SB("vn1", [128, 512], BF16)
        o_b = SB("o_b", [128, 512], BF16)
        ms_b = SB("ms_b", [128, 8], F32)
        og_b = SB("og_b", [128, 512], BF16)
        atile = SB("atile", [128, 1024], BF16)
        ogf_b = SB("ogf_b", [128, 4, 512], BF16)
        xtf = SB("xtf", [128, 1024], F32)
        ffa = SB("ffa", [128, 4, 1024], F32)
        o_t1 = SB("o_t1", [128, 1024], F32)
        rww_t = SB("rww_t", [128, 8], F32)
        rwT_ob = SB("rwT_ob", [8, 128], BF16)
        rwwb_bf = SB("rwwb_bf", [128, 4, 8], BF16)
        sm_t = SB("sm_t", [128, 8], F32)
        rwwb = SB("rwwb", [128, 4, 8], F32)
        rt4 = SB("rt4", [128, 4, 8], F32)
        rt4b = SB("rt4b", [128, 4, 8], F32)
        sm_u = SB("sm_u", [128, 8], F32)
        sm_m = SB("sm_m", [128, 2], F32)

        p0 = PS("p0", [128, 512])
        p1 = PS("p1", [128, 512])
        p2 = PS("p2", [128, 512])
        p3 = PS("p3", [128, 512])
        p4 = PS("p4", [128, 512])
        p5 = PS("p5", [128, 512])
        p6 = PS("p6", [128, 512])
        p7 = PS("p7", [128, 512])
        banks = [p0, p1, p2, p3, p4, p5, p6, p7]

        sLD = SEM("sLD")
        sGD = SEM("sGD")
        sT = SEM("sT")
        sV = SEM("sV")
        sSc = SEM("sSc")
        sCC = SEM("sCC")
        sST = SEM("sST")
        sGP = SEM("sGP")

        block = es.enter_context(nc.Block())

        def program(eng):
            for _s in (sLD, sGD, sT, sV, sSc, sST, sCC, sGP):
                _s.n = 0
            E = {"sync": nc.sync, "gpsimd": nc.gpsimd, "vector": nc.vector,
                 "scalar": nc.scalar, "tensor": nc.tensor}[eng]
            me = eng
            v, sc, gp, sy = nc.vector, nc.scalar, nc.gpsimd, nc.sync

            class _T:
                @staticmethod
                def matmul(*a, **kw):
                    kw.setdefault("skip_group_check", True)
                    return nc.tensor.matmul(*a, **kw)

                @staticmethod
                def transpose(*a, **kw):
                    return nc.tensor.transpose(*a, **kw)
            t = _T()
            p6b = p6[:].bitcast(BF16)
            p4b = p4[:].bitcast(BF16)

            def dma(qeng, out, in_, sem=None, depth=None):
                depth = depth or DMA_DEPTH
                sem = sem or (sGD if qeng == GPS else sLD)
                if me == qeng:
                    E.wait_ge(sem.h, max(0, sem.n - 16 * (depth - 1)))
                    E.dma_start(out=out, in_=in_).then_inc(sem.h, 16)
                sem.n += 16

            def dmaop(qeng, fn, sem=None, depth=None):
                depth = depth or DMA_DEPTH
                sem = sem or (sGD if qeng == GPS else sLD)
                if me == qeng:
                    E.wait_ge(sem.h, max(0, sem.n - 16 * (depth - 1)))
                    fn().then_inc(sem.h, 16)
                sem.n += 16

            own_sem = {VEC: sV, SCL: sSc, GPS: sGP, TEN: sT}

            def op(oeng, fn, sem=None, k=1):
                if me == oeng:
                    if oeng in (VEC, SCL, GPS):
                        E.wait_ge(own_sem[oeng].h, own_sem[oeng].n)
                    ins = fn()
                    if sem is not None:
                        ins.then_inc(sem.h, k)
                if sem is not None:
                    sem.n += k

            def wait(weng, sem, val=None):
                if me == weng:
                    E.wait_ge(sem.h, sem.n if val is None else val)

            def bck(ap, nrep):
                return ap.unsqueeze(1).broadcast_to(
                    [ap.shape[0], nrep, ap.shape[1]])

            def bch(ap, rep):
                return ap.unsqueeze(2).broadcast_to(
                    [ap.shape[0], ap.shape[1], rep])

            for (dst, src) in [(k_triu, c_triu), (k_ustr, c_ustr),
                               (k_strict, c_strict), (k_trilT, c_trilT),
                               (k_sum0, csum0), (k_sum1, csum1),
                               (k_swp, cswp), (k_i128, c_i128),
                               (k_i64s, c_i64s), (k_eps, epsc), (k_dtb, dtb), (k_alog, alog),
                               (k_brr, brr), (k_nw, nwr)]:
                dma(SYN, dst[:], src)
            for (dst, src) in [(W_q, wq), (W_k, wk), (W_v, wv), (W_g, wg)]:
                for dc in range(8):
                    dma(GPS, dst[:, dc, :], src[dc * 128:(dc + 1) * 128, :])
            for dc in range(8):
                dma(GPS, W_sm[:, dc, :], wsm[dc * 128:(dc + 1) * 128, :])
            dma(GPS, B_e[:], bexp)
            wq_g = sGD.n
            wait(GPS, sGD, wq_g)
            kload = sLD.n
            wait(SCL, sLD, kload)
            op(SCL, lambda: sc.activation(k_eA[:], k_alog[:], AF.Exp), sSc)
            op(VEC, lambda: v.memset(S_st[:], 0.0), sV)
            op(VEC, lambda: v.memset(S_bf[:], 0.0), sV)
            op(VEC, lambda: v.memset(vn0[64:128, :], 0.0), sV)
            op(VEC, lambda: v.memset(vn1[0:64, :], 0.0), sV)

            fst = {}

            def front(ti):
                pb = ti % 2
                sl = ti % 2
                if ti + 1 < NTI:
                    wait(GPS, sV)
                    wait(GPS, sSc)
                    dma(GPS, xtm[:, (ti + 1) % 2, :],
                        xTM[(ti + 1) * 128: (ti + 2) * 128, :])
                xload = fst[ti]["xl"]
                wait(SCL, sV)
                wait(SCL, sGD, xload)
                op(SCL, lambda sl=sl: sc.activation(
                    xsq[:], xtm[:, sl, :], AF.Square,
                    accum_out=stat[:, 1:2]), sSc)
                wait(VEC, sGD, xload)
                op(VEC, lambda sl=sl: v.tensor_reduce(
                    stat[:, 0:1], xtm[:, sl, :], AX.X, ALU.add), sV)
                wait(VEC, sSc)
                op(VEC, lambda: v.tensor_scalar(stat[:, 2:3], stat[:, 0:1],
                                                1.0 / D, None, ALU.mult), sV)
                op(VEC, lambda: v.scalar_tensor_tensor(
                    stat[:, 3:4], stat[:, 2:3], 1.0, stat[:, 2:3],
                    ALU.mult, ALU.mult), sV)
                op(VEC, lambda: v.scalar_tensor_tensor(
                    stat[:, 5:6], stat[:, 1:2], 1.0 / D, stat[:, 3:4],
                    ALU.mult, ALU.subtract), sV)
                wait(SCL, sV)
                op(SCL, lambda: sc.activation(
                    stat[:, 6:7], stat[:, 5:6], AF.Sqrt,
                    bias=k_eps[:, 4:5]), sSc)
                wait(VEC, sSc)
                op(VEC, lambda ti=ti: v.reciprocal(rstdc[:, ti:ti + 1],
                                                   stat[:, 6:7]), sV)
                wait(VEC, sT)
                op(VEC, lambda sl=sl: v.tensor_scalar(
                    xctm[:], xtm[:, sl, :], stat[:, 2:3], None,
                    ALU.subtract), sV)
                xc_v = sV.n
                wait(TEN, sV, xc_v)
                i4 = ti % 4
                for half in range(2):
                    wait(TEN, sV)
                    wait(VEC, sLD)
                    for dq in range(4):
                        dc = half * 4 + dq
                        op(TEN, lambda dc=dc, dq=dq: t.transpose(
                            p4b[:, dq * 128:(dq + 1) * 128],
                            xctm[:, dc * 128:(dc + 1) * 128], k_i128[:]),
                           sT if dq == 3 else None)
                    tp = sT.n
                    wait(VEC, sT, tp)
                    op(VEC, lambda half=half, i4=i4: v.tensor_copy(
                        xcb[:, half * 4:(half + 1) * 4,
                            i4 * 128:(i4 + 1) * 128],
                        p4b[:, 0:512].rearrange("p (c n) -> p c n", n=128)),
                       sV)
                xcfm_v = sV.n
                wait(SYN, sV, xcfm_v)
                dma(SYN, xcs[:, :, ti * 128: (ti + 1) * 128],
                    xcb[:, :, i4 * 128:(i4 + 1) * 128])

                if nstg < 2:
                    return
                wait(TEN, sGD, wq_g)
                wait(TEN, sV, xcfm_v)
                for dc in range(8):
                    st_, sp_ = dc == 0, dc == 7
                    lhs = xcb[:, dc, i4 * 128:(i4 + 1) * 128]
                    for (pk, Wt) in ((p0, W_q), (p1, W_k), (p2, W_v),
                                     (p3, W_g)):
                        op(TEN, lambda pk=pk, Wt=Wt, lhs=lhs, dc=dc,
                           st_=st_, sp_=sp_: t.matmul(
                            pk[:], lhs, Wt[:, dc, :], start=st_, stop=sp_),
                           None)
                    op(TEN, lambda lhs=lhs, dc=dc, st_=st_, sp_=sp_:
                       t.matmul(p4[:, 0:24], lhs, W_sm[:, dc, :],
                                start=st_, stop=sp_),
                       sT if sp_ else None)
                proj_t = sT.n

                if nstg < 3:
                    return
                wait(SCL, sT, proj_t)
                wait(SCL, sV)
                op(SCL, lambda ti=ti, pb=pb: sc.activation(
                    beta_b[:, pb, :], p4[:, 0:8], AF.Sigmoid,
                    scale=rstdc[:, ti:ti + 1]), sSc)
                wait(VEC, sT, proj_t)
                op(VEC, lambda ti=ti: v.scalar_tensor_tensor(
                    sm_t[:], p4[:, 8:16], rstdc[:, ti:ti + 1], k_dtb[:],
                    ALU.mult, ALU.add), sV)
                wait(SCL, sV)
                wait(SCL, sGP)
                op(SCL, lambda: sc.activation(sm_u[:], sm_t[:], AF.Abs), sSc)
                op(SCL, lambda: sc.activation(sm_u[:], sm_u[:], AF.Exp,
                                              scale=-1.0), sSc)
                op(SCL, lambda: sc.activation(sm_u[:], sm_u[:], AF.Ln,
                                              bias=k_eps[:, 2:3]), sSc)
                wait(VEC, sSc)
                op(VEC, lambda: v.tensor_scalar_max(sm_t[:], sm_t[:], 0.0), sV)
                op(VEC, lambda: v.tensor_add(sm_t[:], sm_t[:], sm_u[:]), sV)
                op(VEC, lambda pb=pb: v.tensor_mul(dec_b[:, pb, :], sm_t[:],
                                                   k_eA[:]), sV)
                op(VEC, lambda pb=pb: v.tensor_copy(dec_bf[:, pb, :],
                                                    dec_b[:, pb, :]), sV)
                dec_v = sV.n
                if i4 == 0:
                    wait(VEC, sLD)
                op(VEC, lambda ti=ti, i4=i4: v.scalar_tensor_tensor(
                    rwwb[:, i4, :], p4[:, 16:24], rstdc[:, ti:ti + 1],
                    k_brr[:], ALU.mult, ALU.add), sV)
                if i4 == 3:
                    m4a = stat[:, 0:4].unsqueeze(2)
                    m4b = stat[:, 4:8].unsqueeze(2)
                    op(VEC, lambda: v.tensor_reduce(
                        m4a, rwwb[:], AX.X, ALU.max, negate=True), sV)
                    op(VEC, lambda: v.tensor_tensor(
                        rt4[:], rwwb[:], bch(stat[:, 0:4], 8), ALU.add), sV)
                    wait(SCL, sV)
                    op(SCL, lambda: sc.activation(
                        rt4[:].rearrange("p a b -> p (a b)"),
                        rt4[:].rearrange("p a b -> p (a b)"), AF.Exp), sSc)
                    wait(VEC, sSc)
                    op(VEC, lambda: v.tensor_reduce(
                        m4b, rt4[:], AX.X, ALU.add), sV)
                    op(VEC, lambda: v.reciprocal(stat[:, 4:8],
                                                 stat[:, 4:8]), sV)
                    op(VEC, lambda: v.tensor_tensor(
                        rt4[:], rt4[:], bch(stat[:, 4:8], 8), ALU.mult), sV)
                    op(VEC, lambda: v.tensor_reduce(
                        m4a, rt4[:], AX.X, ALU.max), sV)
                    op(VEC, lambda: v.tensor_tensor(
                        rwwb[:], rt4[:], bch(stat[:, 0:4], 8),
                        ALU.is_equal), sV)
                    op(VEC, lambda: v.tensor_scalar(
                        rt4b[:].rearrange("p a b -> p (a b)"),
                        rwwb[:].rearrange("p a b -> p (a b)"),
                        -1.0, 1.0, ALU.mult, ALU.add), sV)
                    op(VEC, lambda: v.tensor_mul(
                        rt4b[:].rearrange("p a b -> p (a b)"),
                        rt4[:].rearrange("p a b -> p (a b)"),
                        rt4b[:].rearrange("p a b -> p (a b)")), sV)
                    op(VEC, lambda: v.tensor_reduce(
                        m4b, rt4b[:], AX.X, ALU.max), sV)
                    op(VEC, lambda: v.tensor_tensor(
                        rt4b[:], rt4b[:], bch(stat[:, 4:8], 8),
                        ALU.is_equal), sV)
                    op(VEC, lambda: v.tensor_add(
                        rwwb[:].rearrange("p a b -> p (a b)"),
                        rwwb[:].rearrange("p a b -> p (a b)"),
                        rt4b[:].rearrange("p a b -> p (a b)")), sV)
                    op(VEC, lambda: v.tensor_mul(
                        rwwb[:].rearrange("p a b -> p (a b)"),
                        rt4[:].rearrange("p a b -> p (a b)"),
                        rwwb[:].rearrange("p a b -> p (a b)")), sV)
                    rww_v = sV.n
                    wait(SYN, sV, rww_v)
                    dma(SYN, rww_d[ti - 3:ti + 1].rearrange(
                        "t p e -> p t e"),
                        rwwb[:])

                if nstg < 4:
                    return
                wait(TEN, sV, dec_v)
                op(TEN, lambda pb=pb: t.matmul(
                    p4[0:64, 24:32], k_triu[0:64, :], dec_bf[0:64, pb, :],
                    start=True, stop=False, tile_position=(0, 0)), None)
                op(TEN, lambda pb=pb: t.matmul(
                    p4[0:64, 32:40], k_ustr[0:64, :], dec_bf[0:64, pb, :],
                    start=False, stop=True, tile_position=(0, 0)), None)
                op(TEN, lambda pb=pb: t.matmul(
                    p4[64:128, 24:32], k_triu[64:128, :], dec_bf[64:128, pb, :],
                    start=True, stop=False, tile_position=(64, 64)), None)
                op(TEN, lambda pb=pb: t.matmul(
                    p4[64:128, 32:40], k_ustr[64:128, :], dec_bf[64:128, pb, :],
                    start=False, stop=True, tile_position=(64, 64)), None)
                op(TEN, lambda pb=pb: t.matmul(
                    p4[:, 40:48], k_sum0[:], dec_bf[:, pb, :],
                    start=True, stop=False), None)
                op(TEN, lambda pb=pb: t.matmul(
                    p4[:, 48:56], k_sum1[:], dec_bf[:, pb, :],
                    start=False, stop=True), sT)
                g_t = sT.n
                wait(SCL, sT, g_t)
                wait(SCL, sV)
                wait(SCL, sGP)
                op(SCL, lambda: sc.activation(eg_b[:], p4[:, 24:32], AF.Exp,
                                              scale=-1.0), sSc)
                op(SCL, lambda: sc.activation(eglg_b[:], p4[:, 32:40],
                                              AF.Exp, scale=-1.0), sSc)
                op(SCL, lambda pb=pb: sc.activation(
                    eglr[:, pb].rearrange("p a b -> p (a b)"),
                    p4[:, 40:56], AF.Exp, scale=-1.0), sSc)
                eg_s = sSc.n
                fst[ti]["eg_s"] = eg_s

                if nstg < 5:
                    return
                wait(SCL, sT, proj_t)
                wait(SCL, sV)
                op(SCL, lambda: sc.activation(xsq[:, 0:512], p0[:],
                                              AF.Square), sSc)
                op(SCL, lambda: sc.activation(xctm[:, 0:512], p1[:],
                                              AF.Square), sSc)
                wait(VEC, sSc)
                op(VEC, lambda: v.tensor_reduce(
                    scl_b[:, 0:8].unsqueeze(2),
                    xsq[:, 0:512].rearrange("p (h k) -> p h k", k=64),
                    AX.X, ALU.add), sV)
                op(VEC, lambda: v.tensor_reduce(
                    scl_b[:, 8:16].unsqueeze(2),
                    xctm[:, 0:512].rearrange("p (h k) -> p h k", k=64),
                    AX.X, ALU.add), sV)
                wait(SCL, sV)
                op(SCL, lambda: sc.activation(stat[:, 8:16], scl_b[:, 0:8],
                                              AF.Sqrt, bias=k_eps[:, 0:1],
                                              scale=64.0), sSc)
                op(SCL, lambda: sc.activation(stat[:, 0:8], scl_b[:, 8:16],
                                              AF.Sqrt, bias=k_eps[:, 1:2]), sSc)
                wait(VEC, sSc)
                op(VEC, lambda: v.reciprocal(scl_b[:, 0:8], stat[:, 8:16]), sV)
                op(VEC, lambda: v.reciprocal(scl_b[:, 8:16], stat[:, 0:8]), sV)
                op(VEC, lambda pb=pb: v.tensor_tensor(
                    qn_b[:, pb].rearrange("p (h k) -> p h k", k=64),
                    p0[:].rearrange("p (h k) -> p h k", k=64),
                    bch(scl_b[:, 0:8], 64),
                    ALU.mult), sV)
                op(VEC, lambda pb=pb: v.tensor_tensor(
                    kn_b[:, pb].rearrange("p (h k) -> p h k", k=64),
                    p1[:].rearrange("p (h k) -> p h k", k=64),
                    bch(scl_b[:, 8:16], 64),
                    ALU.mult), sV)

                wait(VEC, sSc, eg_s)
                op(VEC, lambda ti=ti, pb=pb: v.tensor_scalar(
                    sm_t[:], beta_b[:, pb, :], rstdc[:, ti:ti + 1], None,
                    ALU.mult), sV)
                op(VEC, lambda pb=pb: v.tensor_tensor(
                    vb_b[:, pb].rearrange("p (h k) -> p h k", k=64),
                    p2[:].rearrange("p (h k) -> p h k", k=64),
                    bch(sm_t[:], 64), ALU.mult), sV)
                op(VEC, lambda ti=ti, pb=pb: v.tensor_scalar(
                    gt_b[:, pb], p3[:], rstdc[:, ti:ti + 1], None,
                    ALU.mult), sV)
                op(VEC, lambda pb=pb: v.tensor_tensor(
                    qe_b[:, pb].rearrange("p (h k) -> p h k", k=64),
                    qn_b[:, pb].rearrange("p (h k) -> p h k", k=64),
                    bch(eg_b[:], 64), ALU.mult), sV)
                op(VEC, lambda pb=pb: v.tensor_tensor(
                    ke2[:, pb, :, 0:64],
                    kn_b[:, pb].rearrange("p (h k) -> p h k", k=64),
                    bch(eglg_b[:], 64), ALU.mult), sV)
                wait(GPS, sV)
                wait(GPS, sSc)
                op(GPS, lambda pb=pb: gp.tensor_tensor(
                    ke2[:, pb, :, 64:128],
                    kn_b[:, pb].rearrange("p (h k) -> p h k", k=64),
                    bch(eglg_b[:], 64), ALU.mult), sGP)
                wait(VEC, sGP)
                op(VEC, lambda pb=pb: v.tensor_mul(sm_u[:], beta_b[:, pb, :],
                                                   eg_b[:]), sV)
                wait(GPS, sV)
                op(GPS, lambda pb=pb: gp.tensor_tensor(
                    kbg_b[:, pb].rearrange("p (h k) -> p h k", k=64),
                    kn_b[:, pb].rearrange("p (h k) -> p h k", k=64),
                    bch(sm_u[:], 64), ALU.mult), sGP)
                fst[ti]["scal_v"] = sV.n
                fst[ti]["scal_g"] = sGP.n

            def back(ti):
                pb = ti % 2
                i4 = ti % 4
                if nstg < 6 or "scal_v" not in fst[ti]:
                    return
                qn = qn_b[:, pb]
                kn = kn_b[:, pb]
                qe = qe_b[:, pb]
                vb = vb_b[:, pb]
                kbg = kbg_b[:, pb]
                wait(TEN, sV, fst[ti]["scal_v"])
                wait(TEN, sSc)
                for (srcb, dstb) in ((qn, qnK), (qe, qeK), (kn, knK)):
                    wait(TEN, sV)
                    for hp in range(4):
                        op(TEN, lambda srcb=srcb, hp=hp: t.transpose(
                            p6b[:, hp * 128:(hp + 1) * 128],
                            srcb[:, hp * 128:(hp + 1) * 128], k_i128[:]),
                           sT if hp == 3 else None)
                    tp = sT.n
                    wait(VEC, sT, tp)
                    op(VEC, lambda dstb=dstb: v.tensor_copy(
                        dstb[:].rearrange("p a b -> p (a b)"),
                        p6b[:, 0:512]), sV)
                km_v = sV.n
                for (srcK, dstK) in ((qnK, qnKd), (qeK, qeKd), (knK, knKd)):
                    wait(TEN, sV)
                    wait(TEN, sSc)
                    op(TEN, lambda srcK=srcK: t.matmul(
                        p6[:, 0:512], k_swp[:],
                        srcK[:].rearrange("p a b -> p (a b)"),
                        start=True, stop=True), sT)
                    dp = sT.n
                    wait(SCL, sT, dp)
                    op(SCL, lambda dstK=dstK: sc.copy(
                        dstK[:].rearrange("p a b -> p (a b)"),
                        p6[:, 0:512]), sSc)
                kd_s = sSc.n

                if nstg < 7:
                    return
                op(VEC, lambda pb=pb: v.tensor_tensor(
                    rhsU[:].rearrange("p (h m) -> p h m", m=64),
                    bck(k_ustr[:], 8),
                    bch(dec_b[:, pb, :], 64), ALU.mult), sV)
                wait(GPS, sV)
                op(GPS, lambda pb=pb: gp.tensor_tensor(
                    rhsL[:].rearrange("p (h m) -> p h m", m=64),
                    bck(k_triu[:], 8),
                    bch(dec_b[:, pb, :], 64), ALU.mult), sGP)
                lb_v = sV.n
                lb_g = sGP.n
                wait(TEN, sV, lb_v)
                wait(TEN, sGP, lb_g)
                for sp in range(2):
                    r0 = sp * 64
                    op(TEN, lambda r0=r0: t.matmul(
                        p5[r0:r0 + 64, :], k_triu[r0:r0 + 64, :],
                        rhsU[r0:r0 + 64, :], start=True,
                        stop=True, tile_position=(r0, r0)), None)
                    op(TEN, lambda r0=r0: t.matmul(
                        p7[r0:r0 + 64, :], k_ustr[r0:r0 + 64, :],
                        rhsL[r0:r0 + 64, :], start=True,
                        stop=True, tile_position=(r0, r0)),
                       sT if r0 == 64 else None)
                d_t = sT.n
                wait(SCL, sT, d_t)
                wait(SCL, sV)
                op(SCL, lambda: sc.activation(Lst[:], p5[:], AF.Exp,
                                              scale=-1.0), sSc)
                op(SCL, lambda: sc.activation(LTm[:], p7[:], AF.Exp,
                                              scale=-1.0), sSc)
                exp_s = sSc.n
                wait(VEC, sSc, exp_s)
                op(VEC, lambda: v.tensor_tensor(
                    Lst[:].rearrange("p (h m) -> p h m", m=64),
                    Lst[:].rearrange("p (h m) -> p h m", m=64),
                    bck(k_strict[:], 8), ALU.mult), sV)
                wait(GPS, sSc, exp_s)
                op(GPS, lambda: gp.tensor_tensor(
                    LTm[:].rearrange("p (h m) -> p h m", m=64),
                    LTm[:].rearrange("p (h m) -> p h m", m=64),
                    bck(k_trilT[:], 8), ALU.mult), sGP)
                lm_v = sV.n
                lm_g = sGP.n

                if nstg < 8:
                    return
                wait(TEN, sV, max(km_v, lm_v))
                wait(TEN, sGP, lm_g)
                wait(TEN, sSc, kd_s)
                for h in range(8):
                    hp = h // 2
                    for sp in range(2):
                        r0 = sp * 64
                        knS = knK if (h % 2) == sp else knKd
                        qnS = qnK if (h % 2) == sp else qnKd
                        op(TEN, lambda knS=knS, hp=hp, sp=sp, r0=r0, h=h:
                           t.matmul(
                            p5[r0:r0 + 64, h * 64:(h + 1) * 64],
                            knS[r0:r0 + 64, hp, sp * 64:(sp + 1) * 64],
                            knS[r0:r0 + 64, hp, sp * 64:(sp + 1) * 64],
                            start=(h == 0), stop=(h == 7),
                            tile_position=(r0, r0)), None)
                        op(TEN, lambda knS=knS, qnS=qnS, hp=hp, sp=sp,
                           r0=r0, h=h: t.matmul(
                            p7[r0:r0 + 64, h * 64:(h + 1) * 64],
                            knS[r0:r0 + 64, hp, sp * 64:(sp + 1) * 64],
                            qnS[r0:r0 + 64, hp, sp * 64:(sp + 1) * 64],
                            start=(h == 0), stop=(h == 7),
                            tile_position=(r0, r0)),
                           sT if (h == 7 and sp == 1) else None)
                ga_t = sT.n
                wait(VEC, sT, ga_t)
                op(VEC, lambda pb=pb: v.tensor_tensor(
                    Pk_s[:].rearrange("p (h m) -> p h m", m=64),
                    p5[:].rearrange("p (h m) -> p h m", m=64),
                    bch(beta_b[:, pb, :], 64), ALU.mult), sV)
                op(VEC, lambda: v.tensor_mul(Pk_s[:], Pk_s[:], Lst[:]), sV)
                wait(VEC, sGP, lm_g)
                op(VEC, lambda: v.tensor_mul(attT[:], p7[:], LTm[:]), sV)
                pb_v = sV.n

                if nstg < 9:
                    return
                wait(TEN, sV, pb_v)
                for h in range(8):
                    for sp in range(2):
                        r0 = sp * 64
                        op(TEN, lambda h=h, r0=r0: t.transpose(
                            p6b[r0:r0 + 64, h * 64:(h + 1) * 64],
                            Pk_s[r0:r0 + 64, h * 64:(h + 1) * 64],
                            k_i128[r0:r0 + 64, r0:r0 + 64],
                            tile_position=(r0, r0)),
                           sT if (h == 7 and r0 == 64) else None)
                it_t = sT.n
                wait(VEC, sT, it_t)
                op(VEC, lambda: v.tensor_copy(PkT_s[:], p6b[:, 0:512]), sV)
                op(VEC, lambda: v.scalar_tensor_tensor(
                    Av_s[:].rearrange("p (h m) -> p h m", m=64),
                    Pk_s[:].rearrange("p (h m) -> p h m", m=64),
                    1.0, bck(k_i64s[:], 8), ALU.mult, ALU.add), sV)
                ser_v = sV.n
                ser_c = sSc.n
                for it in range(6):
                    wait(TEN, sV, ser_v)
                    wait(TEN, sSc, ser_c)
                    for h in range(8):
                        hs = slice(h * 64, (h + 1) * 64)
                        for sp in range(2):
                            r0 = sp * 64
                            first = (h == 0)
                            last = (h == 7)
                            if it < 5:
                                op(TEN, lambda hs=hs, r0=r0, first=first,
                                   last=last: t.matmul(
                                    p5[r0:r0 + 64, hs], PkT_s[r0:r0 + 64, hs],
                                    Pk_s[r0:r0 + 64, hs], start=first,
                                    stop=last, tile_position=(r0, r0)), None)
                                op(TEN, lambda hs=hs, r0=r0, first=first,
                                   last=last: t.matmul(
                                    p7[r0:r0 + 64, hs], Pk_s[r0:r0 + 64, hs],
                                    PkT_s[r0:r0 + 64, hs], start=first,
                                    stop=last,
                                    tile_position=(r0, r0)),
                                   sT if (it == 0 and last and sp == 1)
                                   else None)
                            if it >= 1:
                                op(TEN, lambda hs=hs, r0=r0, first=first,
                                   last=last: t.matmul(
                                    p6[r0:r0 + 64, hs], PkT_s[r0:r0 + 64, hs],
                                    Av_s[r0:r0 + 64, hs], start=first,
                                    stop=last, tile_position=(r0, r0)),
                                   sT if (last and sp == 1) else None)
                    si_t = sT.n
                    wait(VEC, sT, si_t)
                    wait(SCL, sT, si_t)
                    if it < 5:
                        op(VEC, lambda: v.tensor_copy(Pk_s[:], p5[:]), sV)
                        op(SCL, lambda: sc.copy(PkT_s[:], p7[:]), sSc)
                    if it >= 1:
                        op(VEC, lambda: v.scalar_tensor_tensor(
                            Av_s[:], p6[:, 0:512], 1.0, Av_s[:],
                            ALU.mult, ALU.add), sV)
                    ser_v = sV.n
                    ser_c = sSc.n
                    wait(TEN, sSc, ser_c)
                wait(TEN, sV, ser_v)
                wait(TEN, sSc, ser_c)
                for h in range(8):
                    for sp in range(2):
                        r0 = sp * 64
                        op(TEN, lambda h=h, r0=r0: t.transpose(
                            p6b[r0:r0 + 64, h * 64:(h + 1) * 64],
                            Av_s[r0:r0 + 64, h * 64:(h + 1) * 64],
                            k_i128[r0:r0 + 64, r0:r0 + 64],
                            tile_position=(r0, r0)),
                           sT if (h == 7 and r0 == 64) else None)
                at_t = sT.n
                wait(VEC, sT, at_t)
                op(VEC, lambda: v.tensor_copy(AvT_s[:], p6b[:, 0:512]), sV)
                avt_v = sV.n

                if nstg < 10:
                    return
                wait(TEN, sV, avt_v)
                wait(TEN, sGP, fst[ti]["scal_g"])
                for h in range(8):
                    hs = slice(h * 64, (h + 1) * 64)
                    for sp in range(2):
                        r0 = sp * 64
                        op(TEN, lambda r0=r0, hs=hs, h=h, vb=vb: t.matmul(
                            p5[r0:r0 + 64, hs], AvT_s[r0:r0 + 64, hs],
                            vb[r0:r0 + 64, hs],
                            start=(h == 0), stop=(h == 7),
                            tile_position=(r0, r0)), None)
                        op(TEN, lambda r0=r0, hs=hs, h=h, kbg=kbg: t.matmul(
                            p7[r0:r0 + 64, hs],
                            kbg[r0:r0 + 64, hs], AvT_s[r0:r0 + 64, hs],
                            start=(h == 0), stop=(h == 7),
                            tile_position=(r0, r0)),
                           sT if (h == 7 and sp == 1) else None)
                uw_t = sT.n
                wait(VEC, sT, uw_t)
                op(VEC, lambda: v.tensor_copy(u_s[:], p5[:]), sV)
                op(VEC, lambda: v.tensor_copy(w_s[:], p7[:]), sV)
                uw_v = sV.n

                if nstg < 11:
                    return
                wait(SCL, sV)
                op(SCL, lambda pb=pb: sc.activation(xsq2[:], gt_b[:, pb],
                                                    AF.Silu), sSc)
                wait(TEN, sGP, fst[ti]["scal_g"])
                wait(VEC, sSc, fst[ti]["eg_s"])
                for sp in range(2):
                    r0 = sp * 64
                    vn = vn0 if sp == 0 else vn1
                    wait(TEN, sV)
                    for h in range(8):
                        hs = slice(h * 64, (h + 1) * 64)
                        op(TEN, lambda hs=hs, r0=r0, h=h: t.matmul(
                            p6[r0:r0 + 64, hs], w_s[r0:r0 + 64, hs],
                            S_bf[r0:r0 + 64, hs],
                            start=(h == 0), stop=(h == 7),
                            tile_position=(r0, r0)),
                           sT if h == 7 else None)
                    vn_t = sT.n
                    wait(VEC, sT, vn_t)
                    op(VEC, lambda r0=r0, vn=vn: v.scalar_tensor_tensor(
                        vn[r0:r0 + 64, :], p6[r0:r0 + 64, 0:512], -1.0,
                        u_s[r0:r0 + 64, :], ALU.mult, ALU.add), sV)
                    vn_v = sV.n
                    wait(TEN, sV, vn_v)
                    for h in range(8):
                        hp = h // 2
                        hs = slice(h * 64, (h + 1) * 64)
                        qeS = qeK if (h % 2) == sp else qeKd
                        op(TEN, lambda qeS=qeS, hs=hs, hp=hp, sp=sp, r0=r0,
                           h=h: t.matmul(
                            p7[r0:r0 + 64, hs],
                            qeS[r0:r0 + 64, hp, sp * 64:(sp + 1) * 64],
                            S_bf[r0:r0 + 64, hs],
                            start=(h == 0), stop=False,
                            tile_position=(r0, r0)), None)
                        op(TEN, lambda hs=hs, r0=r0, h=h, vn=vn: t.matmul(
                            p7[r0:r0 + 64, hs], attT[r0:r0 + 64, hs],
                            vn[r0:r0 + 64, hs], start=False,
                            stop=(h == 7), tile_position=(r0, r0)), None)
                        op(TEN, lambda hs=hs, h=h, vn=vn, pb=pb: t.matmul(
                            p5[:, hs], ke2[:, pb, h, :], vn[:, hs],
                            start=(h == 0), stop=(h == 7)),
                           sT if h == 7 else None)
                    su_t = sT.n
                    wait(SCL, sT, su_t)
                    op(SCL, lambda r0=r0: sc.copy(
                        o_b[r0:r0 + 64, :], p7[r0:r0 + 64, :]), sSc)
                    wait(VEC, sT, su_t)
                    op(VEC, lambda sp=sp, pb=pb: v.tensor_tensor(
                        Stmp[:].rearrange("p (a b) -> p a b", b=64),
                        S_st[:].rearrange("p (a b) -> p a b", b=64),
                        bch(eglr[:, pb, sp, :], 64),
                        ALU.mult), sV)
                    op(VEC, lambda: v.tensor_add(S_st[:], Stmp[:],
                                                 p5[:, 0:512]), sV)
                    wait(SCL, sV)
                    op(SCL, lambda: sc.copy(S_bf[:], S_st[:]), sSc)
                    wait(TEN, sSc)

                if nstg < 12:
                    return
                wait(SCL, sV)
                op(SCL, lambda: sc.activation(xsq[:, 0:512], o_b[:],
                                              AF.Square), sSc)
                wait(VEC, sSc)
                op(VEC, lambda: v.tensor_reduce(
                    ms_b[:].unsqueeze(2),
                    xsq[:, 0:512].rearrange("p (h k) -> p h k", k=64),
                    AX.X, ALU.add), sV)
                wait(SCL, sV)
                op(SCL, lambda: sc.activation(ms_b[:], ms_b[:], AF.Sqrt,
                                              bias=k_eps[:, 4:5],
                                              scale=1.0 / 64), sSc)
                wait(VEC, sSc)
                op(VEC, lambda: v.reciprocal(ms_b[:], ms_b[:]), sV)
                op(VEC, lambda: v.tensor_mul(og_b[:], o_b[:],
                                             xsq2[:]), sV)
                op(VEC, lambda: v.tensor_tensor(
                    og_b[:].rearrange("p (h k) -> p h k", k=64),
                    og_b[:].rearrange("p (h k) -> p h k", k=64),
                    bch(ms_b[:], 64), ALU.mult), sV)
                wait(GPS, sV)
                op(GPS, lambda: gp.tensor_tensor(
                    og_b[:].rearrange("p (h k) -> p h k", k=64),
                    og_b[:].rearrange("p (h k) -> p h k", k=64),
                    bck(k_nw[:], 8), ALU.mult), sGP)
                og_v = sV.n
                og_g = sGP.n
                wait(TEN, sV, og_v)
                wait(TEN, sGP, og_g)
                for hp in range(4):
                    op(TEN, lambda hp=hp: t.transpose(
                        p6b[:, hp * 128:(hp + 1) * 128],
                        og_b[:, hp * 128:(hp + 1) * 128], k_i128[:]),
                       sT if hp == 3 else None)
                ogt_t = sT.n
                wait(SCL, sT, ogt_t)
                wait(SCL, sLD)
                op(SCL, lambda i4=i4: sc.copy(
                    ogf_b[:, :, i4 * 128:(i4 + 1) * 128],
                    p6b[:, 0:512].rearrange("p (a b) -> p a b", b=128)), sSc)
                if i4 == 3:
                    t0 = (ti // 4) * 512
                    ogf_v2 = sSc.n
                    wait(SYN, sSc, ogf_v2)
                    for hp in range(4):
                        dma(SYN, ogall[hp * 128:(hp + 1) * 128, t0:t0 + 512],
                            ogf_b[:, hp, :])

            NTI = min(NT, 4 * nblk) if nblk < NT else NT
            wait(GPS, sV)
            wait(GPS, sSc)
            dma(GPS, xtm[:, 0, :], xTM[0:128, :])
            for ti in range(NTI):
                fst[ti] = {"xl": sGD.n}
                front(ti)
                if ti >= 1:
                    back(ti - 1)
                    del fst[ti - 1]
            if NTI >= 1:
                back(NTI - 1)
            if phases < 2:
                op(VEC, lambda: v.memset(o_t1[:], 0.0), sV)
                wait(SYN, sV)
                for ob in range(NOB):
                    for tt in range(4):
                        dma(SYN, outY[ob * 512 + tt * 128: ob * 512 + (tt + 1) * 128, :],
                            o_t1[:], sST)
                wait(SYN, sST)
                return
            wait(GPS, sLD)
            rank = None
            if me == GPS:
                rank = gp.cc_rank(PAIR_GROUPS)

            def ccin_dma():
                src = ogall[:, 0:OWNT]
                src = dataclasses.replace(src, offset=(1 - rank) * OWNT)
                return gp.dma_start(out=cc_in.rearrange("(a b) -> a b", a=512),
                                    in_=src)
            dmaop(GPS, ccin_dma)
            wait(GPS, sGD)
            op(GPS, lambda: gp.collective_compute(
                "AllGather", ALU.bypass, replica_groups=PAIR_GROUPS,
                ins=[cc_in], outs=[cc_out]), sCC)
            wait(GPS, sCC)
            wait(SYN, sCC)
            wait(TEN, sCC)

            rank_s = None
            if me == SYN:
                rank_s = sy.cc_rank(PAIR_GROUPS)

            def cp_ogown():
                src = dataclasses.replace(ogall[:, 0:OWNT], offset=rank_s * OWNT)
                return sy.dma_start(out=og_own, in_=src)
            dmaop(SYN, cp_ogown)

            def cp_ogpar():
                src = dataclasses.replace(cc_out[0:512 * OWNT],
                                          offset=(1 - rank_s) * (512 * OWNT))
                return sy.dma_start(out=og_par.rearrange("a b -> (a b)"), in_=src)
            dmaop(SYN, cp_ogpar)

            def cp_xcs():
                src = dataclasses.replace(xcs[:, :, 0:OWNT], offset=rank_s * OWNT)
                return sy.dma_start(out=xcs_own, in_=src)
            dmaop(SYN, cp_xcs)

            def cp_rww():
                src = dataclasses.replace(
                    rww_d[0:NT // 2], offset=rank_s * (NT // 2) * (128 * 8))
                return sy.dma_start(out=rww_own, in_=src)
            dmaop(SYN, cp_rww)
            wait(SYN, sLD)

            if phases < 3:
                op(VEC, lambda: v.memset(o_t1[:], 0.0), sV)
                wait(SYN, sV)
                for ob in range(NOB):
                    for tt in range(4):
                        dma(SYN, outY[ob * 512 + tt * 128: ob * 512 + (tt + 1) * 128, :],
                            o_t1[:], sST)
                wait(SYN, sST)
                return
            for dc in range(8):
                dma(GPS, W_e[:, 0, dc, :], wo[dc * 128:(dc + 1) * 128, :])
            wo_g = sGD.n
            wait(TEN, sGD, wo_g)
            for ob in range(NOB):
                wait(SYN, sT)
                for hp in range(4):
                    dma(SYN, xcb[:, hp, :],
                        og_own[hp * 128:(hp + 1) * 128, ob * 512:(ob + 1) * 512])
                    dma(SYN, xcb[:, 4 + hp, :],
                        og_par[hp * 128:(hp + 1) * 128, ob * 512:(ob + 1) * 512])
                ogl = sLD.n
                wait(TEN, sLD, ogl)
                for tt in range(4):
                    ot = ob * 4 + tt
                    wait(TEN, sV)
                    for dc in range(8):
                        for jh in range(2):
                            op(TEN, lambda tt=tt, jh=jh, dc=dc: t.matmul(
                                (p0 if jh == 0 else p1)[:],
                                xcb[:, dc, tt * 128:(tt + 1) * 128],
                                W_e[:, 0, dc, jh * 512:(jh + 1) * 512],
                                start=(dc == 0), stop=(dc == 7)),
                               sT if (dc == 7 and jh == 1) else None)
                    wo_t = sT.n
                    wait(VEC, sT, wo_t)
                    wait(VEC, sST)
                    op(VEC, lambda: v.tensor_copy(og_b[:], p0[:]), sV)
                    op(VEC, lambda: v.tensor_copy(o_b[:], p1[:]), sV)
                    at_v = sV.n
                    wait(SYN, sV, at_v)
                    dma(SYN, attn_d[ot, :, 0:512], og_b[:], sST)
                    dma(SYN, attn_d[ot, :, 512:1024], o_b[:], sST)

            if phases < 4:
                op(VEC, lambda: v.memset(o_t1[:], 0.0), sV)
                wait(SYN, sV)
                for ob in range(NOB):
                    for tt in range(4):
                        dma(SYN, outY[ob * 512 + tt * 128: ob * 512 + (tt + 1) * 128, :],
                            o_t1[:], sST)
                wait(SYN, sST)
                return
            prevmm = sT.n
            wuser = [prevmm, prevmm]
            for ob in range(NOB):
                wait(SYN, sT, prevmm)
                for dc in range(8):
                    dma(SYN, xcb[:, dc, :], xcs_own[:, dc, ob * 512:(ob + 1) * 512])
                for tt in range(4):
                    dma(SYN, rwwb[:, tt, :], rww_own[ob * 4 + tt])
                xcl = sLD.n
                wait(VEC, sLD, xcl)
                op(VEC, lambda: v.tensor_copy(
                    rwwb_bf[:].rearrange("p a b -> p (a b)"),
                    rwwb[:].rearrange("p a b -> p (a b)")), sV)
                ev_half = [sV.n, sV.n]
                mmdone = []
                for e in range(8):
                    wait(GPS, sT, mmdone[e - 2] if e >= 2 else wuser[e % 2])
                    for dc in range(8):
                        def ld_we(e=e, dc=dc):
                            return gp.dma_start(
                                out=W_e[:, e % 2, dc, :],
                                in_=weT[e, dc * 128:(dc + 1) * 128, :])
                        dmaop(GPS, ld_we)
                    wsm_n = sGD.n
                    wait(TEN, sGD, wsm_n)
                    if e == 0:
                        wait(TEN, sLD, xcl)
                    for th in range(2):
                        wait(TEN, sV, ev_half[th])
                        for tt in (th * 2, th * 2 + 1):
                            for dc in range(8):
                                for jh in range(2):
                                    bank = banks[tt * 2 + jh]
                                    op(TEN, lambda bank=bank, tt=tt, dc=dc,
                                       jh=jh, e=e: t.matmul(
                                        bank[:],
                                        xcb[:, dc, tt * 128:(tt + 1) * 128],
                                        W_e[:, e % 2, dc, jh * 512:(jh + 1) * 512],
                                        start=(dc == 0), stop=(dc == 7)),
                                       sT if (tt == th * 2 + 1 and dc == 7
                                              and jh == 1) else None)
                        mm_half = sT.n
                        wait(VEC, sT, mm_half)
                        for tt in (th * 2, th * 2 + 1):
                            for jh in range(2):
                                bank = banks[tt * 2 + jh]
                                dst = ffa[:, tt, jh * 512:(jh + 1) * 512]
                                if e == 0:
                                    op(VEC, lambda bank=bank, dst=dst, tt=tt,
                                       e=e: v.tensor_scalar(
                                        dst, bank[:], rwwb[:, tt, e:e + 1],
                                        None, ALU.mult), sV)
                                else:
                                    op(VEC, lambda bank=bank, dst=dst, tt=tt,
                                       e=e: v.scalar_tensor_tensor(
                                        dst, bank[:], rwwb[:, tt, e:e + 1],
                                        dst, ALU.mult, ALU.add), sV)
                        ev_half[th] = sV.n
                    mmdone.append(sT.n)
                wuser = [mmdone[6], mmdone[7]]
                for tt in range(4):
                    ot = ob * 4 + tt

                    def ld_at(ot=ot):
                        return sy.dma_start(out=atile[:], in_=attn_d[ot])
                    wait(SYN, sV)
                    dmaop(SYN, ld_at)
                    dma(SYN, xtf[:],
                        xOWN[ob * 512 + tt * 128: ob * 512 + (tt + 1) * 128, :])
                    xfl = sLD.n
                    wait(TEN, sV, ev_half[1])
                    op(TEN, lambda tt=tt: t.transpose(
                        p6b[0:8, 0:128], rwwb_bf[:, tt, :], k_i128[:]), sT)
                    rwt_t = sT.n
                    wait(SCL, sT, rwt_t)
                    op(SCL, lambda: sc.copy(rwT_ob[0:8, :],
                                            p6b[0:8, 0:128]), sSc)
                    rwg = sSc.n
                    wait(TEN, sSc, rwg)
                    wait(TEN, sV, ev_half[tt // 2])
                    for jh in range(2):
                        op(TEN, lambda tt=tt, jh=jh: t.matmul(
                            banks[tt * 2 + jh][:], rwT_ob[0:8, :],
                            B_e[:, jh * 512:(jh + 1) * 512], start=True,
                            stop=True), sT if jh == 1 else None)
                    fb_t = sT.n
                    wait(VEC, sT, fb_t)
                    wait(VEC, sLD, xfl)
                    wait(VEC, sST)
                    for jh in range(2):
                        op(VEC, lambda jh=jh, tt=tt: v.tensor_add(
                            o_t1[:, jh * 512:(jh + 1) * 512],
                            ffa[:, tt, jh * 512:(jh + 1) * 512],
                            banks[tt * 2 + jh][:]), sV)
                    op(VEC, lambda tt=tt: v.tensor_add(
                        o_t1[:].rearrange("p (a b) -> p a b", a=2),
                        o_t1[:].rearrange("p (a b) -> p a b", a=2),
                        atile[:].rearrange("p (a b) -> p a b", a=2)), sV)
                    op(VEC, lambda: v.tensor_add(o_t1[:], o_t1[:], xtf[:]), sV)
                    out_v = sV.n
                    wait(SYN, sV, out_v)
                    dma(SYN, outY[ob * 512 + tt * 128: ob * 512 + (tt + 1) * 128, :],
                        o_t1[:], sST)
                prevmm = sT.n
            wait(SYN, sST)

        @block.sync
        def _(_e):
            program("sync")

        @block.gpsimd
        def _(_e):
            program("gpsimd")

        @block.vector
        def _(_e):
            program("vector")

        @block.scalar
        def _(_e):
            program("scalar")

        @block.tensor
        def _(_e):
            program("tensor")

    return nc


def _consts():
    tri = np.triu(np.ones((C, C), np.float32))
    ust = np.tril(np.ones((C, C), np.float32), -1)
    stc = np.tril(np.ones((C, C), np.float32), -1)
    tlt = np.triu(np.ones((C, C), np.float32))
    bf = ml_dtypes.bfloat16
    return {
        "c_triu": np.tile(tri, (2, 1)).astype(bf),
        "c_ustr": np.tile(ust, (2, 1)).astype(bf),
        "c_strict": np.tile(stc, (2, 1)).astype(bf),
        "c_trilT": np.tile(tlt, (2, 1)).astype(bf),
        "csum0": np.vstack([np.ones((64, 128)), np.zeros((64, 128))]).astype(bf),
        "csum1": np.vstack([np.zeros((64, 128)), np.ones((64, 128))]).astype(bf),
        "cswp": np.roll(np.eye(128), 64, axis=0).astype(bf),
        "c_i128": np.eye(128).astype(bf),
        "c_i64s": np.tile(np.eye(64, dtype=np.float32), (2, 1)),
    }


def kernel_impl(inputs, n_cores, runner):
    x = np.asarray(inputs["L0_attn_x"], np.float32)
    Bb, T, _ = x.shape
    Wq = np.asarray(inputs["L0_attn_Wq"], np.float32)
    Wk = np.asarray(inputs["L0_attn_Wk"], np.float32)
    Wv = np.asarray(inputs["L0_attn_Wv"], np.float32)
    Wg = np.asarray(inputs["L0_attn_Wg"], np.float32)
    Wo = np.asarray(inputs["L0_attn_Wo"], np.float32)
    Wb = np.asarray(inputs["L0_attn_Wb"], np.float32)
    Wa = np.asarray(inputs["L0_attn_Wa"], np.float32)
    dt_bias = np.asarray(inputs["L0_attn_dt_bias"], np.float32)
    A_log = np.asarray(inputs["L0_attn_A_log"], np.float32)
    norm_w = np.asarray(inputs["L0_attn_norm_w"], np.float32)
    We = np.asarray(inputs["L0_ffn_W_experts"], np.float32)
    be = np.asarray(inputs["L0_ffn_b_experts"], np.float32)
    Wr = np.asarray(inputs["L0_ffn_W_router"], np.float32)
    br = np.asarray(inputs["L0_ffn_b_router"], np.float32)

    consts = _consts()
    weT = np.ascontiguousarray(We.transpose(0, 2, 1))
    in_maps = []
    for cidx in range(n_cores):
        b, g = cidx // 2, cidx % 2
        hs = slice(g * 512, (g + 1) * 512)
        h8 = slice(g * 8, (g + 1) * 8)
        im = {
            "xTM": np.ascontiguousarray(x[b]),
            "wq": np.ascontiguousarray(Wq[:, hs]),
            "wk": np.ascontiguousarray(Wk[:, hs]),
            "wv": np.ascontiguousarray(Wv[:, hs]),
            "wg": np.ascontiguousarray(Wg[:, hs]),
            "wsm": np.ascontiguousarray(np.concatenate(
                [Wb[:, h8], Wa[:, h8], Wr.T], axis=1)),
            "wo": np.ascontiguousarray(np.concatenate(
                [Wo[g * 512:(g + 1) * 512], Wo[(1 - g) * 512:(2 - g) * 512]],
                axis=0)),
            "weT": weT,
            "bexp": be,
            "dtb": np.ascontiguousarray(np.broadcast_to(dt_bias[h8], (128, 8))),
            "alog": np.ascontiguousarray(np.broadcast_to(A_log[h8], (128, 8))),
            "brr": np.ascontiguousarray(np.broadcast_to(br, (128, 8))),
            "nwr": np.ascontiguousarray(np.broadcast_to(norm_w, (128, 64))),
        }
        im["epsc"] = np.ascontiguousarray(np.broadcast_to(
            np.array([64e-6, 1e-6, 1.0, 0.0, 1e-5, 0.0], np.float32),
            (128, 6)))
        im["xOWN"] = np.ascontiguousarray(x[b, g * (T // 2):(g + 1) * (T // 2)])
        im.update(consts)
        in_maps.append(im)

    nc = build_nc(T=T, n_cores=n_cores)
    res = runner(nc, in_maps)
    out = np.zeros((Bb, T, D), np.float32)
    half = T // 2
    for cidx in range(n_cores):
        b, g = cidx // 2, cidx % 2
        out[b, g * half:(g + 1) * half] = res[cidx]["outY"]
    return out


def kernel(**inputs):
    def runner(nc, in_maps):
        r = run_bass_kernel_spmd(nc, in_maps, core_ids=list(range(len(in_maps))))
        return r.results
    return kernel_impl(inputs, 8, runner)
